# version 01
# baseline (speedup 1.0000x reference)
import numpy as np
import ml_dtypes
import concourse.bass as bass
import concourse.mybir as mybir
from concourse.bass_utils import run_bass_kernel_spmd
from contextlib import ExitStack

F32 = mybir.dt.float32
BF16 = mybir.dt.bfloat16
AF = mybir.ActivationFunctionType
OP = mybir.AluOpType
AX = mybir.AxisListType
bf16 = ml_dtypes.bfloat16

T = 4096
D = 1024
DFF = 4096
NH, HD = 16, 64
NT = T // 128
NC = D // 128
NE = 8

SCRATCH = {
    "s_attn0": ([NT, 128, D], BF16),
    "s_ffn0": ([NT, 128, D], BF16),
    "s_x1": ([NT, 128, D], F32),
    "s_xnt1": ([NT, NC, 128, 128], BF16),
    "s_qt": ([NT, NC, 128, 128], BF16),
    "s_y1": ([NT, 128, D], BF16),
}
PHASES = {
    "ph0": (["c_ident", "L0_attn_b_v", "L0_attn_b_o", "L0_ffn_b2",
             "L0_ffn_b1", "L0_ffn_quantum_phases", "L0_ffn_quantum_weights_real",
             "L0_ffn_quantum_weights_imag", "L0_attn_W_basis",
             "L1_ffn_router_w", "L1_ffn_router_b", "L1_attn_ln_g", "L1_attn_ln_b"],
            [], []),
    "ph1": (["x"], [], []),
    "ph2": (["L0_attn_W_v", "L0_attn_W_o"], [], []),
    "ph3": (["c_mask"], [], ["s_attn0"]),
    "ph4": (["L0_ffn_W1", "L0_ffn_W2"], [], ["s_ffn0"]),
    "ph4b": (["x"], ["s_attn0", "s_ffn0"], ["s_x1"]),
    "ph5": (["L1_attn_Wq", "L1_attn_bq", "L1_attn_Wk", "L1_attn_bk",
             "L1_attn_Wv", "L1_attn_bv"], ["s_x1"], ["s_xnt1", "s_qt"]),
    "ph6": ([], ["s_qt"], ["s_y1"]),
    "ph7": ([], [], []),
    "ph8": (["L1_ffn_shared_w1", "L1_ffn_shared_b1", "L1_ffn_shared_w2",
             "L1_ffn_shared_b2", "L1_ffn_spec_w1", "L1_ffn_spec_b1",
             "L1_ffn_spec_w2", "L1_ffn_spec_b2"],
            ["s_x1", "s_y1", "s_xnt1"], []),
}
PHASE_ORDER = ["ph0", "ph1", "ph2", "ph3", "ph4", "ph4b", "ph5", "ph6", "ph7", "ph8"]
GROUPS = [PHASE_ORDER]

INPUT_SPECS = {
    "L0_attn_W_basis": (64, 1024), "L0_attn_W_v": (1024, 1024),
    "L0_attn_b_v": (1024,), "L0_attn_W_o": (1024, 1024), "L0_attn_b_o": (1024,),
    "L0_ffn_W1": (1024, 4096), "L0_ffn_b1": (4096,),
    "L0_ffn_quantum_phases": (4096,), "L0_ffn_quantum_weights_real": (4096,),
    "L0_ffn_quantum_weights_imag": (4096,), "L0_ffn_W2": (4096, 1024),
    "L0_ffn_b2": (1024,), "L1_attn_ln_g": (1024,), "L1_attn_ln_b": (1024,),
    "L1_attn_Wq": (1024, 1024), "L1_attn_bq": (1024,),
    "L1_attn_Wk": (1024, 1024), "L1_attn_bk": (1024,),
    "L1_attn_Wv": (1024, 1024), "L1_attn_bv": (1024,),
    "L1_ffn_shared_w1": (4096, 1024), "L1_ffn_shared_b1": (4096,),
    "L1_ffn_shared_w2": (1024, 4096), "L1_ffn_shared_b2": (1024,),
    "L1_ffn_router_w": (8, 1024), "L1_ffn_router_b": (8,),
    "L1_ffn_spec_w1": (8, 4096, 1024), "L1_ffn_spec_b1": (8, 4096),
    "L1_ffn_spec_w2": (8, 1024, 4096), "L1_ffn_spec_b2": (8, 1024),
    "x": (T, D), "c_ident": (128, 128), "c_mask": (128, 128),
}


class Sem:
    def __init__(self, h):
        self.h = h
        self.n = 0


class Q:

    def __init__(self, name):
        self.name = name
        self.thunks = []

    def raw(self, fn):
        self.thunks.append(fn)

    def wait(self, sem, val=None):
        v = sem.n if val is None else int(val)
        self.raw(lambda e, s=sem.h, v=v: e.wait_ge(s, v))

    def op(self, meth, inc=None, amt=1, **kw):
        if inc is not None:
            inc.n += amt

        def fn(e, meth=meth, kw=kw, inc=inc, amt=amt):
            inst = getattr(e, meth)(**kw)
            if inc is not None:
                inst.then_inc(inc.h, amt)
        self.raw(fn)

    def dma(self, out, in_, inc, **kw):
        inc.n += 16

        def fn(e, out=out, in_=in_, inc=inc, kw=kw):
            e.dma_start(out=out, in_=in_, **kw).then_inc(inc.h, 16)
        self.raw(fn)


class Ctx:
    pass


def declare_group_tensors(nc, group_phases):
    later = []
    idx = max(PHASE_ORDER.index(p) for p in group_phases)
    for p in PHASE_ORDER[idx + 1:]:
        later.append(p)
    later_reads = set()
    for ph in later:
        later_reads |= set(PHASES[ph][1])
    reads, writes, uses = set(), set(), set()
    for ph in group_phases:
        u, r, w = PHASES[ph]
        uses |= set(u)
        reads |= {s for s in r if s not in writes}
        writes |= set(w)
    t = {}
    input_names = []
    for name in sorted(uses):
        t[name] = nc.dram_tensor(name, list(INPUT_SPECS[name]), F32,
                                 kind="ExternalInput").ap()
        input_names.append(name)
    for sname in sorted(reads | writes):
        shape, dt = SCRATCH[sname]
        if sname in reads and sname not in writes:
            kind = "ExternalInput"
            input_names.append(sname)
        elif sname in later_reads:
            kind = "ExternalOutput"
        else:
            kind = "Internal"
        t[sname] = nc.dram_tensor(sname, shape, dt, kind=kind).ap()
    if "ph8" in group_phases:
        t["out"] = nc.dram_tensor("out", [T, D], F32, kind="ExternalOutput").ap()
    return t, input_names


def alloc_sbuf(nc, ctx, stack):
    A = stack.enter_context(nc.sbuf_tensor("arenaA", [128, 32768], BF16))
    B = stack.enter_context(nc.sbuf_tensor("arenaB", [128, 32768], BF16))
    C = stack.enter_context(nc.sbuf_tensor("arenaC", [128, 32768], BF16))
    Dm = stack.enter_context(nc.sbuf_tensor("arenaD", [128, 7936], BF16))
    PS = stack.enter_context(nc.psum_tensor("ps8", [128, 4096], F32))
    ctx.A, ctx.B, ctx.C, ctx.D, ctx.PS = A[:], B[:], C[:], Dm[:], PS[:]
    ctx.PSbf = ctx.PS.bitcast(BF16)
    ctx.id_bf = ctx.D[:, 0:128]
    ctx.id_f32 = ctx.D[:, 128:384].bitcast(F32)
    ctx.ones_row = ctx.D[0:1, 384:512]
    ctx.logits = ctx.D[:, 640:1152].bitcast(F32)
    ctx.comb = ctx.D[:, 1152:1664].bitcast(F32)
    ctx.rows = ctx.D[:, 1664:5760]
    co = 5760

    def colf32(off, n):
        return ctx.D[:, co + off:co + off + 2 * n].bitcast(F32)
    ctx.b1T = colf32(0, 32)
    ctx.sinT = colf32(64, 32)
    ctx.cosT = colf32(128, 32)
    ctx.aT = colf32(192, 32)
    ctx.gT = colf32(256, 8)
    ctx.qbias = colf32(272, 8)
    ctx.kbias = colf32(288, 8)
    ctx.stat = colf32(304, 16)
    ctx.red = colf32(336, 16)
    ctx.tmpc = colf32(368, 64)
    ctx.eb1T = colf32(496, 32)
    ctx.routwT = ctx.D[:, co + 560:co + 624]
    ctx.bT_bf = ctx.D[:, co + 624:co + 632]
    ctx.spare = ctx.D[:, 6784:7936]
    ctx.out0_evac = ctx.spare[:, 0:1024]


def barrier(ctx):
    bar = ctx.sem_bar
    for q in ctx.queues:
        bar.n += 1
        q.raw(lambda e, s=bar.h: e.nop().then_inc(s, 1))
    tgt = bar.n
    for q in ctx.queues:
        q.raw(lambda e, s=bar.h, v=tgt: e.wait_ge(s, v))


def emit_ln(ctx, x_ap, out_bf, sq, eps=1e-5):
    v, s = ctx.qv, ctx.qs
    ve, ac = ctx.s_ve, ctx.s_ac
    st = ctx.stat
    s.wait(ve)
    s.op("activation", inc=ac, out=sq, in_=x_ap, func=AF.Square,
         accum_out=st[:, 0:1], amt=2)
    v.op("tensor_reduce", inc=ve, out=st[:, 1:2], in_=x_ap, axis=AX.X, op=OP.add)
    s.wait(ve)
    s.op("activation", out=st[:, 4:5], in_=st[:, 1:2], func=AF.Square, scale=1.0 / D)
    s.op("activation", inc=ac, out=st[:, 8:9], in_=st[:, 1:2], func=AF.Copy,
         scale=-1.0 / D)
    v.wait(ac)
    v.op("tensor_scalar", inc=ve, out=st[:, 5:6], in0=st[:, 0:1],
         scalar1=1.0 / D, scalar2=st[:, 4:5], op0=OP.mult, op1=OP.subtract)
    s.wait(ve)
    s.op("activation", inc=ac, out=st[:, 6:7], in_=st[:, 5:6], func=AF.Sqrt,
         bias=st[:, 15:16])
    v.wait(ac)
    v.op("reciprocal", inc=ve, out=st[:, 7:8], in_=st[:, 6:7])
    v.wait(ve)
    v.op("tensor_scalar", inc=ve, out=out_bf, in0=x_ap, scalar1=st[:, 8:9],
         scalar2=st[:, 7:8], op0=OP.add, op1=OP.mult)



def ph0(ctx):
    t, v, s, g, sy = ctx.qt, ctx.qv, ctx.qs, ctx.qg, ctx.qy
    ld, pe, ve, ac = ctx.s_ld, ctx.s_pe, ctx.s_ve, ctx.s_ac
    ten = ctx.ten
    C = ctx.C
    f32C = C.bitcast(F32)

    sy.dma(ctx.id_f32, ten["c_ident"], ld)
    g.dma(ctx.id_bf, ten["c_ident"], ld)
    g.dma(ctx.rows[0:1, 0:1024], ten["L0_attn_b_v"][None, :], ld)
    g.dma(ctx.rows[0:1, 1024:2048], ten["L0_attn_b_o"][None, :], ld)
    g.dma(ctx.rows[0:1, 2048:3072], ten["L0_ffn_b2"][None, :], ld)
    g.dma(ctx.spare[0:1, 1024:1032], ten["L1_ffn_router_b"][None, :], ld)
    g.dma(C[0:8, 0:1024], ten["L1_ffn_router_w"], ld)
    g.dma(C[0:64, 1024:2048], ten["L0_attn_W_basis"], ld)
    th_st = f32C[0:32, 1024:1152]
    wr_st = f32C[0:32, 1152:1280]
    wi_st = f32C[0:32, 1280:1408]
    b1_st = f32C[0:32, 1408:1536]
    g_st = f32C[0:8, 1536:1664]
    b_st = f32C[0:8, 1664:1792]
    sy.dma(th_st, ten["L0_ffn_quantum_phases"].rearrange("(c p) -> c p", p=128), ld)
    sy.dma(wr_st, ten["L0_ffn_quantum_weights_real"].rearrange("(c p) -> c p", p=128), ld)
    sy.dma(wi_st, ten["L0_ffn_quantum_weights_imag"].rearrange("(c p) -> c p", p=128), ld)
    sy.dma(b1_st, ten["L0_ffn_b1"].rearrange("(c p) -> c p", p=128), ld)
    sy.dma(g_st, ten["L1_attn_ln_g"].rearrange("(c p) -> c p", p=128), ld)
    sy.dma(b_st, ten["L1_attn_ln_b"].rearrange("(c p) -> c p", p=128), ld)

    v.wait(ld)
    v.op("memset", ap=ctx.ones_row, constant=1.0)
    v.op("memset", inc=ve, ap=ctx.stat[:, 15:16], constant=1e-5)

    def vstep(meth, **kw):
        v.op(meth, inc=ve, **kw)
        v.wait(ve)

    PSf = ctx.PS
    t.wait(ld)
    t.op("transpose", out=PSf[:, 0:32], in_=th_st, identity=ctx.id_f32[0:32, 0:32])
    t.op("transpose", out=PSf[:, 32:64], in_=wr_st, identity=ctx.id_f32[0:32, 0:32])
    t.op("transpose", out=PSf[:, 64:96], in_=wi_st, identity=ctx.id_f32[0:32, 0:32])
    t.op("transpose", out=PSf[:, 96:128], in_=b1_st, identity=ctx.id_f32[0:32, 0:32])
    t.op("transpose", out=PSf[:, 128:136], in_=g_st, identity=ctx.id_f32[0:8, 0:8])
    t.op("transpose", inc=pe, out=PSf[:, 136:144], in_=b_st, identity=ctx.id_f32[0:8, 0:8])
    v.wait(pe)
    thT = ctx.tmpc[:, 0:32]
    wiT = ctx.tmpc[:, 32:64]
    wrT = ctx.tmpc[:, 96:128] if False else ctx.tmpc[:, 0:0]
    wrT = ctx.red[:, 0:16]
    wrT = ctx.aT
    v.op("tensor_copy", out=thT, in_=PSf[:, 0:32])
    v.op("tensor_copy", out=wrT, in_=PSf[:, 32:64])
    v.op("tensor_copy", out=wiT, in_=PSf[:, 64:96])
    v.op("tensor_copy", out=ctx.b1T, in_=PSf[:, 96:128])
    v.op("tensor_copy", out=ctx.gT, in_=PSf[:, 128:136])
    v.op("tensor_copy", inc=ve, out=ctx.bT_bf, in_=PSf[:, 136:144])
    t.wait(ve)
    v.wait(ve)

    twopi = float(2 * np.pi)
    vstep("scalar_tensor_tensor", out=thT, in0=ctx.b1T, scalar=0.1,
          in1=thT, op0=OP.mult, op1=OP.add)
    msk = ctx.sinT
    for it in range(8):
        vstep("tensor_single_scalar", out=msk, in_=thT, scalar=float(np.pi), op=OP.is_ge)
        vstep("scalar_tensor_tensor", out=thT, in0=msk, scalar=-twopi,
              in1=thT, op0=OP.mult, op1=OP.add)
        vstep("tensor_single_scalar", out=msk, in_=thT, scalar=float(-np.pi), op=OP.is_lt)
        vstep("scalar_tensor_tensor", out=thT, in0=msk, scalar=twopi,
              in1=thT, op0=OP.mult, op1=OP.add)
    s.wait(ve)
    s.op("activation", inc=ac, out=ctx.sinT, in_=thT, func=AF.Sin)
    s.wait(ac)
    s.op("activation", inc=ac, out=ctx.cosT, in_=thT, func=AF.Sin, scale=0.5)
    v.wait(ac)
    vstep("tensor_tensor", out=ctx.cosT, in0=ctx.cosT, in1=ctx.cosT, op=OP.mult)
    vstep("tensor_scalar", out=ctx.cosT, in0=ctx.cosT, scalar1=-2.0, scalar2=1.0,
          op0=OP.mult, op1=OP.add)
    qp = thT
    vstep("tensor_tensor", out=qp, in0=ctx.cosT, in1=wrT, op=OP.mult)
    vstep("tensor_tensor", out=wrT, in0=ctx.sinT, in1=wiT, op=OP.mult)
    vstep("tensor_tensor", out=qp, in0=qp, in1=wrT, op=OP.add)
    vstep("tensor_tensor", out=ctx.aT, in0=qp, in1=wiT, op=OP.mult)
    vstep("tensor_single_scalar", out=ctx.aT, in_=ctx.aT, scalar=0.1, op=OP.mult)

    for c in range(NC):
        t.op("transpose", out=ctx.PSbf[:, 2048 + c * 8:2048 + (c + 1) * 8],
             in_=C[0:8, c * 128:(c + 1) * 128],
             identity=ctx.id_bf[0:8, 0:8], inc=(pe if c == NC - 1 else None))
    v.wait(pe)
    v.op("tensor_copy", inc=ve, out=ctx.routwT, in_=ctx.PSbf[:, 2048:2112])
    for c in range(NC):
        t.op("transpose", out=ctx.PSbf[:, 4096 + c * 64:4096 + (c + 1) * 64],
             in_=C[0:64, 1024 + c * 128:1024 + (c + 1) * 128],
             identity=ctx.id_bf[0:64, 0:64], inc=(pe if c == NC - 1 else None))
    v.wait(pe)
    v.op("tensor_copy", inc=ve, out=C[:, 24576:25088], in_=ctx.PSbf[:, 4096:4608])
    t.wait(ve)


def ph1(ctx):
    t, v, s, g, sy = ctx.qt, ctx.qv, ctx.qs, ctx.qg, ctx.qy
    ld, pe, ve, ac = ctx.s_ld, ctx.s_pe, ctx.s_ve, ctx.s_ac
    ten = ctx.ten
    A, C = ctx.A, ctx.C
    f32C = C.bitcast(F32)
    sq = f32C[:, 0:1024]
    xr = [f32C[:, 1024:2048], f32C[:, 2048:3072]]
    xn_bf = C[:, 6144:7168]
    ve_done = [0, 0]

    for tt in range(NT):
        sl = tt % 2
        if tt >= 2:
            sy.wait(ctx.s_ve, ve_done[sl])
        sy.dma(xr[sl], ten["x"][tt * 128:(tt + 1) * 128, :], ld)
        s.wait(ld)
        v.wait(ld)
        emit_ln(ctx, xr[sl], xn_bf, sq)
        ve_done[sl] = ctx.s_ve.n
        t.wait(ve)
        for c in range(NC):
            t.op("transpose", out=ctx.PSbf[:, c * 128:(c + 1) * 128],
                 in_=xn_bf[:, c * 128:(c + 1) * 128],
                 identity=ctx.id_bf, inc=(pe if c == NC - 1 else None))
        v.wait(pe)
        for c in range(NC):
            v.op("tensor_copy", out=A[:, c * 4096 + tt * 128:c * 4096 + (tt + 1) * 128],
                 in_=ctx.PSbf[:, c * 128:(c + 1) * 128],
                 inc=(ve if c == NC - 1 else None))
        t.wait(ve)


def _load_and_transpose_1024(ctx, wname, stg, dest):
    t, v, g = ctx.qt, ctx.qv, ctx.qg
    ld, pe, ve = ctx.s_ld, ctx.s_pe, ctx.s_ve
    g.wait(pe, getattr(ctx, "stg_free_pe", 0))
    for c in range(NC):
        g.dma(stg[:, c * 1024:(c + 1) * 1024],
              ctx.ten[wname][c * 128:(c + 1) * 128, :], ld)
    t.wait(ld)
    for dc in range(NC):
        for oc in range(NC):
            t.op("transpose", out=ctx.PSbf[:, oc * 128:(oc + 1) * 128],
                 in_=stg[:, oc * 1024 + dc * 128:oc * 1024 + (dc + 1) * 128],
                 identity=ctx.id_bf, inc=(pe if oc == NC - 1 else None))
        v.wait(pe)
        v.op("tensor_copy", inc=ve, out=dest[:, dc * 1024:(dc + 1) * 1024],
             in_=ctx.PSbf[:, 0:1024])
        t.wait(ve)
    ctx.stg_free_pe = ctx.s_pe.n


def ph2(ctx):
    t, v, s, g, sy = ctx.qt, ctx.qv, ctx.qs, ctx.qg, ctx.qy
    ld, pe, ve, ac = ctx.s_ld, ctx.s_pe, ctx.s_ve, ctx.s_ac
    ten = ctx.ten
    A, B, C = ctx.A, ctx.B, ctx.C
    stg = C[:, 0:8192]
    WvT = C[:, 8192:16384]
    WoT = C[:, 16384:24576]
    WbT = C[:, 24576:25088]
    latT = C[:, 25088:29184]
    r_bv = ctx.rows[0:1, 0:1024]

    _load_and_transpose_1024(ctx, "L0_attn_W_v", stg, WvT)

    for tt in range(NT):
        for dc in range(NC):
            for h in range(2):
                t.op("matmul", out=ctx.PS[:, h * 512:(h + 1) * 512],
                     lhsT=A[:, dc * 4096 + tt * 128:dc * 4096 + (tt + 1) * 128],
                     rhs=WvT[:, dc * 1024 + h * 512:dc * 1024 + (h + 1) * 512],
                     start=(dc == 0), stop=False)
        for h in range(2):
            t.op("matmul", out=ctx.PS[:, h * 512:(h + 1) * 512],
                 lhsT=ctx.ones_row, rhs=r_bv[:, h * 512:(h + 1) * 512],
                 start=False, stop=True, inc=(pe if h == 1 else None))
        s.wait(pe)
        s.op("activation", inc=ac, out=B[:, tt * 1024:(tt + 1) * 1024],
             in_=ctx.PS[:, 0:1024], func=AF.Copy)
        t.wait(ac)

    _load_and_transpose_1024(ctx, "L0_attn_W_o", stg, WoT)

    for tg in range(8):
        for dc in range(NC):
            t.op("matmul", out=ctx.PS[0:64, 0:512],
                 lhsT=WbT[:, dc * 64:(dc + 1) * 64],
                 rhs=A[:, dc * 4096 + tg * 512:dc * 4096 + (tg + 1) * 512],
                 start=(dc == 0), stop=(dc == NC - 1),
                 inc=(pe if dc == NC - 1 else None))
        v.wait(pe)
        v.op("tensor_copy", inc=ve, out=latT[0:64, tg * 512:(tg + 1) * 512],
             in_=ctx.PS[0:64, 0:512])
        t.wait(ve)


def ph3(ctx):
    t, v, s, g, sy = ctx.qt, ctx.qv, ctx.qs, ctx.qg, ctx.qy
    ld, pe, ve, ac, so = ctx.s_ld, ctx.s_pe, ctx.s_ve, ctx.s_ac, ctx.s_so
    ten = ctx.ten
    A, B, C = ctx.A, ctx.B, ctx.C
    f32C = C.bitcast(F32)
    maskbig = f32C[:, 0:128]
    att = C[:, 8192 + 0:8192 + 4096] if False else C[:, 0:0]
    att = C[:, 4096:8192]
    attT = C[:, 29184 + 0:29184] if False else None
    attT = C[:, 31232 + 0:31232] if False else None
    attT = C[:, 8192:12288]
    WoT = C[:, 16384:24576]
    latT = C[:, 25088:29184]
    y_bf = C[:, 29184:30208]
    yT = C[:, 30208:31232]
    out0 = ctx.out0_evac
    r_bo = ctx.rows[0:1, 1024:2048]
    st_done = 0

    sy.dma(maskbig, ten["c_mask"], ld)
    v.wait(ld)

    for qi in range(NT):
        nk = qi + 1
        klen = nk * 128
        nch = (klen + 511) // 512
        for c in range(nch):
            cols = min(512, klen - c * 512)
            t.op("matmul", out=ctx.PS[:, c * 512:c * 512 + cols],
                 lhsT=latT[0:64, qi * 128:(qi + 1) * 128],
                 rhs=latT[0:64, c * 512:c * 512 + cols],
                 start=True, stop=True, inc=(pe if c == nch - 1 else None))
        v.wait(pe)
        dg = ctx.PS[:, qi * 128:klen]
        v.op("tensor_tensor", inc=ve, out=dg, in0=dg, in1=maskbig, op=OP.add)
        v.wait(ve)
        v.op("tensor_reduce", inc=ve, out=ctx.red[:, 0:1], in_=ctx.PS[:, 0:klen],
             axis=AX.X, op=OP.max, negate=True)
        v.wait(ve)
        v.op("tensor_single_scalar", inc=ve, out=ctx.red[:, 0:1],
             in_=ctx.red[:, 0:1], scalar=0.125, op=OP.mult)
        s.wait(ve)
        s.op("activation", inc=ac, out=att[:, 0:klen], in_=ctx.PS[:, 0:klen],
             func=AF.Exp, scale=0.125, bias=ctx.red[:, 0:1],
             accum_out=ctx.red[:, 1:2], amt=2)
        v.wait(ac)
        v.op("reciprocal", inc=ve, out=ctx.red[:, 2:3], in_=ctx.red[:, 1:2])
        t.wait(ac)
        for kc in range(nk):
            bank = kc // 8
            slot = kc % 8
            t.op("transpose",
                 out=ctx.PSbf[:, bank * 1024 + slot * 128:bank * 1024 + (slot + 1) * 128],
                 in_=att[:, kc * 128:(kc + 1) * 128], identity=ctx.id_bf,
                 inc=(pe if kc == nk - 1 else None))
        nb = (nk + 7) // 8
        v.wait(pe)
        for b in range(nb):
            cols = min(1024, klen - b * 1024)
            v.op("tensor_copy", out=attT[:, b * 1024:b * 1024 + cols],
                 in_=ctx.PSbf[:, b * 1024:b * 1024 + cols],
                 inc=(ve if b == nb - 1 else None))
        t.wait(ve)
        for kc in range(nk):
            for h in range(2):
                t.op("matmul", out=ctx.PS[:, 2048 + h * 512:2048 + (h + 1) * 512],
                     lhsT=attT[:, kc * 128:(kc + 1) * 128],
                     rhs=B[:, kc * 1024 + h * 512:kc * 1024 + (h + 1) * 512],
                     start=(kc == 0), stop=(kc == nk - 1),
                     inc=(pe if (kc == nk - 1 and h == 1) else None))
        s.wait(pe)
        s.wait(ve)
        s.op("activation", inc=ac, out=y_bf, in_=ctx.PS[:, 2048:3072],
             func=AF.Copy, scale=ctx.red[:, 2:3])
        t.wait(ac)
        for c in range(NC):
            t.op("transpose", out=ctx.PSbf[:, c * 128:(c + 1) * 128],
                 in_=y_bf[:, c * 128:(c + 1) * 128], identity=ctx.id_bf,
                 inc=(pe if c == NC - 1 else None))
        v.wait(pe)
        v.op("tensor_copy", inc=ve, out=yT, in_=ctx.PSbf[:, 0:1024])
        t.wait(ve)
        for c in range(NC):
            for h in range(2):
                t.op("matmul", out=ctx.PS[:, 2048 + h * 512:2048 + (h + 1) * 512],
                     lhsT=yT[:, c * 128:(c + 1) * 128],
                     rhs=WoT[:, c * 1024 + h * 512:c * 1024 + (h + 1) * 512],
                     start=(c == 0), stop=False)
        for h in range(2):
            t.op("matmul", out=ctx.PS[:, 2048 + h * 512:2048 + (h + 1) * 512],
                 lhsT=ctx.ones_row, rhs=r_bo[:, h * 512:(h + 1) * 512],
                 start=False, stop=True, inc=(pe if h == 1 else None))
        if qi >= 1:
            s.wait(so, st_done)
        s.wait(pe)
        s.op("activation", inc=ac, out=out0, in_=ctx.PS[:, 2048:3072], func=AF.Copy)
        g.wait(ac)
        g.dma(ten["s_attn0"][qi], out0, so)
        st_done = ctx.s_so.n
        t.wait(ac)
    g.wait(so)


def ph4(ctx):
    t, v, s, g, sy = ctx.qt, ctx.qv, ctx.qs, ctx.qg, ctx.qy
    ld, pe, ve, ac, so = ctx.s_ld, ctx.s_pe, ctx.s_ve, ctx.s_ac, ctx.s_so
    ten = ctx.ten
    A, B, C = ctx.A, ctx.B, ctx.C
    for c in range(NC):
        g.dma(B[:, c * 4096:(c + 1) * 4096],
              ten["L0_ffn_W1"][c * 128:(c + 1) * 128, :], ld)
    for fc in range(32):
        g.dma(C[:, fc * 1024:(fc + 1) * 1024],
              ten["L0_ffn_W2"][fc * 128:(fc + 1) * 128, :], ld)
    t.wait(ld)

    halfpi = ctx.red[:, 8:9]
    v.op("memset", inc=ve, ap=halfpi, constant=float(np.pi / 2))
    s.wait(ve)
    actT = [ctx.spare[:, 0:256], ctx.spare[:, 256:512]]
    snbuf = ctx.spare[:, 512:1024].bitcast(F32)
    csbuf = ctx.D[:, 6784 + 1024:6784 + 1152].bitcast(F32)
    csbuf = ctx.rows[:, 0:512].bitcast(F32)
    outev = ctx.rows[:, 3072 + 0:3072 + 1024]
    r_b2 = ctx.rows[0:1, 2048:3072]

    for tg in range(16):
        tcol = tg * 256
        for fs in range(32):
            hb = fs % 2
            hps = ctx.PS[:, hb * 512:hb * 512 + 256]
            for dc in range(NC):
                t.op("matmul", out=hps,
                     lhsT=B[:, dc * 4096 + fs * 128:dc * 4096 + (fs + 1) * 128],
                     rhs=A[:, dc * 4096 + tcol:dc * 4096 + tcol + 256],
                     start=(dc == 0), stop=(dc == NC - 1),
                     inc=(pe if dc == NC - 1 else None))
            sincol = ctx.sinT[:, fs:fs + 1]
            coscol = ctx.cosT[:, fs:fs + 1]
            acol = ctx.aT[:, fs:fs + 1]
            b1col = ctx.b1T[:, fs:fs + 1]
            s.wait(pe)
            s.op("activation", out=snbuf, in_=hps, func=AF.Sin, scale=0.1)
            s.op("activation", inc=ac, out=csbuf, in_=hps, func=AF.Sin,
                 scale=-0.1, bias=halfpi)
            v.wait(ac)
            v.op("tensor_scalar", out=snbuf, in0=snbuf, scalar1=coscol,
                 scalar2=None, op0=OP.mult)
            v.op("scalar_tensor_tensor", out=snbuf, in0=csbuf, scalar=sincol,
                 in1=snbuf, op0=OP.mult, op1=OP.add)
            v.op("tensor_scalar", out=snbuf, in0=snbuf, scalar1=acol,
                 scalar2=None, op0=OP.mult)
            v.op("tensor_tensor", inc=ve, out=snbuf, in0=snbuf, in1=hps, op=OP.add)
            s.wait(ve)
            s.op("activation", inc=ac, out=actT[hb], in_=snbuf, func=AF.Relu,
                 bias=b1col)
            t.wait(ac)
            for sub in range(2):
                for h in range(2):
                    t.op("matmul",
                         out=ctx.PS[:, 2048 + sub * 1024 + h * 512:2048 + sub * 1024 + (h + 1) * 512],
                         lhsT=actT[hb][:, sub * 128:(sub + 1) * 128],
                         rhs=C[:, fs * 1024 + h * 512:fs * 1024 + (h + 1) * 512],
                         start=(fs == 0), stop=False)
        for sub in range(2):
            for h in range(2):
                t.op("matmul",
                     out=ctx.PS[:, 2048 + sub * 1024 + h * 512:2048 + sub * 1024 + (h + 1) * 512],
                     lhsT=ctx.ones_row, rhs=r_b2[:, h * 512:(h + 1) * 512],
                     start=False, stop=True,
                     inc=(pe if (sub == 1 and h == 1) else None))
        s.wait(pe)
        for sub in range(2):
            if tg > 0 or sub > 0:
                s.wait(so)
            s.op("activation", inc=ac, out=outev,
                 in_=ctx.PS[:, 2048 + sub * 1024:3072 + sub * 1024], func=AF.Copy)
            g.wait(ac)
            g.dma(ten["s_ffn0"][tg * 2 + sub], outev, so)
        t.wait(ac)
    g.wait(so)


def ph4b(ctx):
    t, v, s, g, sy = ctx.qt, ctx.qv, ctx.qs, ctx.qg, ctx.qy
    ld, ve, so = ctx.s_ld, ctx.s_ve, ctx.s_so
    ten = ctx.ten
    B = ctx.B
    f32B = B.bitcast(F32)
    xr = [f32B[:, 0:1024], f32B[:, 1024:2048]]
    ar = [B[:, 4096:5120], B[:, 5120:6144]]
    fr = [B[:, 6144:7168], B[:, 7168:8192]]
    x1r = [f32B[:, 4096:5120], f32B[:, 5120:6144]]
    ve_done = [0, 0]
    so_done = [0, 0]
    for tt in range(NT):
        sl = tt % 2
        if tt >= 2:
            sy.wait(ve, ve_done[sl])
        sy.dma(xr[sl], ten["x"][tt * 128:(tt + 1) * 128, :], ld)
        sy.dma(ar[sl], ten["s_attn0"][tt], ld)
        sy.dma(fr[sl], ten["s_ffn0"][tt], ld)
        v.wait(ld)
        if tt >= 2:
            v.wait(ctx.s_so, so_done[sl])
        v.op("tensor_tensor", out=xr[sl], in0=xr[sl], in1=ar[sl], op=OP.add)
        v.op("tensor_tensor", inc=ve, out=x1r[sl], in0=xr[sl], in1=fr[sl], op=OP.add)
        ve_done[sl] = ctx.s_ve.n
        g.wait(ve)
        g.dma(ten["s_x1"][tt], x1r[sl], ctx.s_so)
        so_done[sl] = ctx.s_so.n
    g.wait(ctx.s_so)


def ph5(ctx):
    t, v, s, g, sy = ctx.qt, ctx.qv, ctx.qs, ctx.qg, ctx.qy
    ld, pe, ve, ac, so = ctx.s_ld, ctx.s_pe, ctx.s_ve, ctx.s_ac, ctx.s_so
    ten = ctx.ten
    A, B, C = ctx.A, ctx.B, ctx.C
    Wq = C[:, 0:8192]
    Wk = C[:, 8192:16384]
    Wv = C[:, 16384:24576]
    xn_bf = C[:, 24576:25600]
    xnc_bf = C[:, 25600:26624]
    xncT = C[:, 26624:27648]
    xnT_t = C[:, 27648:28672]
    QT_t = C[:, 28672:29696]
    sq = C[:, 29696:31744].bitcast(F32)
    bvp_row = ctx.spare[0:1, 0:1024]
    x1r = [ctx.rows[:, 0:2048].bitcast(F32), ctx.rows[:, 2048:4096].bitcast(F32)]

    for nm, dst in (("L1_attn_Wq", Wq), ("L1_attn_Wk", Wk), ("L1_attn_Wv", Wv)):
        for c in range(NC):
            g.dma(dst[:, c * 1024:(c + 1) * 1024],
                  ten[nm][c * 128:(c + 1) * 128, :], ld)
    t.wait(ld)

    brow = C[0:1, 31744:32768]
    names = [("L1_attn_bq", Wq, "q"), ("L1_attn_bk", Wk, "k"),
             ("L1_attn_bv", Wv, "v")]
    for bn, W, which in names:
        g.wait(pe, ctx.s_pe.n)
        g.dma(brow, ten[bn][None, :], ld)
        t.wait(ld)
        for c in range(NC):
            for h in range(2):
                t.op("matmul", out=ctx.PS[0:1, h * 512:(h + 1) * 512],
                     lhsT=ctx.bT_bf[:, c:c + 1],
                     rhs=W[:, c * 1024 + h * 512:c * 1024 + (h + 1) * 512],
                     start=(c == 0), stop=False)
        for h in range(2):
            t.op("matmul", out=ctx.PS[0:1, h * 512:(h + 1) * 512],
                 lhsT=ctx.ones_row[0:1, 0:1], rhs=brow[:, h * 512:(h + 1) * 512],
                 start=False, stop=True, inc=(pe if h == 1 else None))
        if which == "v":
            s.wait(pe)
            s.op("activation", inc=ac, out=bvp_row, in_=ctx.PS[0:1, 0:1024],
                 func=AF.Copy)
            t.wait(ac)
        else:
            v.wait(pe)
            v.op("tensor_copy", inc=ve, out=sq[0:1, 0:1024], in_=ctx.PS[0:1, 0:1024])
            t.wait(ve)
            for c in range(NC):
                t.op("transpose", out=ctx.PS[:, c * 512:c * 512 + 1],
                     in_=sq[0:1, c * 128:(c + 1) * 128],
                     identity=ctx.id_f32[0:1, 0:1],
                     inc=(pe if c == NC - 1 else None))
            v.wait(pe)
            dstcol = ctx.qbias if which == "q" else ctx.kbias
            for c in range(NC):
                v.op("tensor_copy", out=dstcol[:, c:c + 1],
                     in_=ctx.PS[:, c * 512:c * 512 + 1],
                     inc=(ve if c == NC - 1 else None))
            if which == "q":
                v.wait(ve)
                v.op("tensor_single_scalar", inc=ve, out=dstcol, in_=dstcol,
                     scalar=0.125, op=OP.mult)
            t.wait(ve)
    v.wait(ac)
    for W in (Wq, Wk, Wv):
        for c in range(NC):
            v.op("tensor_scalar", out=W[:, c * 1024:(c + 1) * 1024],
                 in0=W[:, c * 1024:(c + 1) * 1024],
                 scalar1=ctx.gT[:, c:c + 1], scalar2=None, op0=OP.mult,
                 inc=(ve if c == NC - 1 else None))
    t.wait(ve)

    ve_done = [0, 0]
    for tt in range(NT):
        sl = tt % 2
        if tt >= 2:
            sy.wait(ve, ve_done[sl])
        sy.dma(x1r[sl], ten["s_x1"][tt], ld)
        s.wait(ld)
        v.wait(ld)
        emit_ln(ctx, x1r[sl], xn_bf, sq)
        ve_done[sl] = ctx.s_ve.n
        s.wait(ve)
        emit_ln(ctx, xn_bf, xnc_bf, sq)
        t.wait(ve)
        for c in range(NC):
            t.op("transpose", out=ctx.PSbf[:, c * 128:(c + 1) * 128],
                 in_=xn_bf[:, c * 128:(c + 1) * 128], identity=ctx.id_bf)
        for c in range(NC):
            t.op("transpose", out=ctx.PSbf[:, 1024 + c * 128:1024 + (c + 1) * 128],
                 in_=xnc_bf[:, c * 128:(c + 1) * 128], identity=ctx.id_bf,
                 inc=(pe if c == NC - 1 else None))
        v.wait(pe)
        v.op("tensor_copy", out=xnT_t, in_=ctx.PSbf[:, 0:1024])
        v.op("tensor_copy", inc=ve, out=xncT, in_=ctx.PSbf[:, 1024:2048])
        g.wait(ve)
        g.dma(ten["s_xnt1"][tt].rearrange("c p f -> p c f"),
              xnT_t.rearrange("p (c f) -> p c f", c=NC), so)
        t.wait(ve)
        for c in range(NC):
            t.op("matmul", out=ctx.PS[:, 2048:2056],
                 lhsT=xnT_t[:, c * 128:(c + 1) * 128],
                 rhs=ctx.routwT[:, c * 8:(c + 1) * 8],
                 start=(c == 0), stop=False)
        t.op("matmul", out=ctx.PS[:, 2048:2056], lhsT=ctx.ones_row,
             rhs=ctx.spare[0:1, 1024:1032], start=False, stop=True, inc=pe)
        v.wait(pe)
        v.op("tensor_copy", inc=ve, out=ctx.logits[:, tt * 8:(tt + 1) * 8],
             in_=ctx.PS[:, 2048:2056])
        for W, bias, is_q in ((Wq, ctx.qbias, True), (Wk, ctx.kbias, False)):
            for oc in range(NC):
                pso = ctx.PS[:, 2560 + (oc % 2) * 128:2560 + (oc % 2) * 128 + 128]
                for c in range(NC):
                    t.op("matmul", out=pso,
                         lhsT=W[:, c * 1024 + oc * 128:c * 1024 + (oc + 1) * 128],
                         rhs=xncT[:, c * 128:(c + 1) * 128],
                         start=(c == 0), stop=(c == NC - 1),
                         inc=(pe if c == NC - 1 else None))
                s.wait(pe)
                if is_q:
                    s.op("activation", inc=ac, out=QT_t[:, oc * 128:(oc + 1) * 128],
                         in_=pso, func=AF.Identity, scale=0.125, bias=bias[:, oc:oc + 1])
                else:
                    s.op("activation", inc=ac,
                         out=A[:, oc * 4096 + tt * 128:oc * 4096 + (tt + 1) * 128],
                         in_=pso, func=AF.Identity, bias=bias[:, oc:oc + 1])
                t.wait(ac)
            if is_q:
                g.wait(ac)
                g.dma(ten["s_qt"][tt].rearrange("c p f -> p c f"),
                      QT_t.rearrange("p (c f) -> p c f", c=NC), so)
        for c in range(NC):
            for h in range(2):
                t.op("matmul", out=ctx.PS[:, h * 512:(h + 1) * 512],
                     lhsT=xncT[:, c * 128:(c + 1) * 128],
                     rhs=Wv[:, c * 1024 + h * 512:c * 1024 + (h + 1) * 512],
                     start=(c == 0), stop=False)
        for h in range(2):
            t.op("matmul", out=ctx.PS[:, h * 512:(h + 1) * 512],
                 lhsT=ctx.ones_row, rhs=bvp_row[:, h * 512:(h + 1) * 512],
                 start=False, stop=True, inc=(pe if h == 1 else None))
        s.wait(pe)
        s.op("activation", inc=ac, out=B[:, tt * 1024:(tt + 1) * 1024],
             in_=ctx.PS[:, 0:1024], func=AF.Copy)
        t.wait(ac)
    g.wait(so)


def ph6(ctx):
    t, v, s, g, sy = ctx.qt, ctx.qv, ctx.qs, ctx.qg, ctx.qy
    ld, pe, ve, ac, so = ctx.s_ld, ctx.s_pe, ctx.s_ve, ctx.s_ac, ctx.s_so
    ten = ctx.ten
    A, B, C = ctx.A, ctx.B, ctx.C
    QTr = [C[:, 0:1024], C[:, 1024:2048]]
    y_t = C[:, 2048:3072]
    att = C[:, 4096:8192]
    attT = C[:, 8192:12288]
    pe_done = [0, 0]
    st_done = 0
    for qi in range(NT):
        sl = qi % 2
        if qi >= 2:
            sy.wait(pe, pe_done[sl])
        sy.dma(QTr[sl], ten["s_qt"][qi].rearrange("c p f -> p c f"), ld)
        t.wait(ld)
        for h in range(NH):
            c = h // 2
            half = (h % 2) * 64
            for kc in range(8):
                t.op("matmul", out=ctx.PS[:, kc * 512:(kc + 1) * 512],
                     lhsT=QTr[sl][half:half + 64, c * 128:(c + 1) * 128],
                     rhs=A[half:half + 64, c * 4096 + kc * 512:c * 4096 + (kc + 1) * 512],
                     start=True, stop=True, inc=(pe if kc == 7 else None))
            if h == NH - 1:
                pe_done[sl] = ctx.s_pe.n
            v.wait(pe)
            v.op("tensor_reduce", inc=ve, out=ctx.red[:, 0:1], in_=ctx.PS,
                 axis=AX.X, op=OP.max, negate=True)
            s.wait(ve)
            s.op("activation", inc=ac, out=att, in_=ctx.PS, func=AF.Exp,
                 bias=ctx.red[:, 0:1], accum_out=ctx.red[:, 1:2], amt=2)
            v.wait(ac)
            v.op("reciprocal", inc=ve, out=ctx.red[:, 2:3], in_=ctx.red[:, 1:2])
            t.wait(ac)
            for kc in range(32):
                bank = kc // 8
                slot = kc % 8
                t.op("transpose",
                     out=ctx.PSbf[:, bank * 1024 + slot * 128:bank * 1024 + (slot + 1) * 128],
                     in_=att[:, kc * 128:(kc + 1) * 128], identity=ctx.id_bf,
                     inc=(pe if kc == 31 else None))
            v.wait(pe)
            for b in range(4):
                v.op("tensor_copy", out=attT[:, b * 1024:(b + 1) * 1024],
                     in_=ctx.PSbf[:, b * 1024:(b + 1) * 1024],
                     inc=(ve if b == 3 else None))
            t.wait(ve)
            for kc in range(32):
                t.op("matmul", out=ctx.PS[:, 2048:2112],
                     lhsT=attT[:, kc * 128:(kc + 1) * 128],
                     rhs=B[:, kc * 1024 + h * 64:kc * 1024 + (h + 1) * 64],
                     start=(kc == 0), stop=(kc == 31),
                     inc=(pe if kc == 31 else None))
            s.wait(pe)
            s.wait(ve)
            if h == 0 and qi >= 1:
                s.wait(so, st_done)
            s.op("activation", inc=ac, out=y_t[:, h * 64:(h + 1) * 64],
                 in_=ctx.PS[:, 2048:2112], func=AF.Copy, scale=ctx.red[:, 2:3])
            t.wait(ac)
        g.wait(ac)
        g.dma(ten["s_y1"][qi], y_t, so)
        st_done = ctx.s_so.n
    g.wait(so)


def ph7(ctx):
    t, v, s, g = ctx.qt, ctx.qv, ctx.qs, ctx.qg
    pe, ve, ac = ctx.s_pe, ctx.s_ve, ctx.s_ac
    lg = ctx.logits
    cb = ctx.comb
    l3 = lg.rearrange("p (t e) -> p t e", e=8)
    c3 = cb.rearrange("p (t e) -> p t e", e=8)
    m1 = ctx.tmpc[:, 0:32]
    m2 = ctx.tmpc[:, 32:64]
    sp = ctx.spare[:, 0:256].bitcast(F32)
    den = sp[:, 0:32]
    w1n = sp[:, 32:64]
    w2n = sp[:, 64:96]
    eq1 = sp[:, 96:128]

    ve = ctx.s_ve

    def vstep(meth, **kw):
        v.op(meth, inc=ve, **kw)
        v.wait(ve)

    s.op("activation", inc=ac, out=lg, in_=lg, func=AF.Exp)
    v.wait(ac)
    vstep("tensor_reduce", out=m2.rearrange("p (t o) -> p t o", o=1),
          in_=l3, axis=AX.X, op=OP.add)
    vstep("reciprocal", out=m2, in_=m2)
    for e in range(8):
        v.op("tensor_tensor", inc=(ve if e == 7 else None),
             out=l3[:, :, e], in0=l3[:, :, e], in1=m2, op=OP.mult)
    v.wait(ve)
    vstep("tensor_reduce", out=m1.rearrange("p (t o) -> p t o", o=1),
          in_=l3, axis=AX.X, op=OP.max)
    for e in range(8):
        vstep("tensor_tensor", out=c3[:, :, e], in0=l3[:, :, e], in1=m1, op=OP.is_equal)
        vstep("scalar_tensor_tensor", out=c3[:, :, e], in0=c3[:, :, e],
              scalar=-1e9, in1=l3[:, :, e], op0=OP.mult, op1=OP.add)
    vstep("tensor_reduce", out=m2.rearrange("p (t o) -> p t o", o=1),
          in_=c3, axis=AX.X, op=OP.max)
    vstep("tensor_tensor", out=den, in0=m1, in1=m2, op=OP.add)
    vstep("reciprocal", out=den, in_=den)
    vstep("tensor_tensor", out=w1n, in0=m1, in1=den, op=OP.mult)
    vstep("tensor_tensor", out=w2n, in0=m2, in1=den, op=OP.mult)
    for e in range(8):
        vstep("tensor_tensor", out=eq1, in0=l3[:, :, e], in1=m1, op=OP.is_equal)
        vstep("tensor_tensor", out=eq1, in0=eq1, in1=w1n, op=OP.mult)
        vstep("tensor_tensor", out=c3[:, :, e], in0=c3[:, :, e], in1=m2, op=OP.is_equal)
        vstep("tensor_tensor", out=c3[:, :, e], in0=c3[:, :, e], in1=w2n, op=OP.mult)
        vstep("tensor_tensor", out=c3[:, :, e], in0=c3[:, :, e], in1=eq1, op=OP.add)


def ph8(ctx):
    t, v, s, g, sy = ctx.qt, ctx.qv, ctx.qs, ctx.qg, ctx.qy
    ld, pe, ve, ac, so = ctx.s_ld, ctx.s_pe, ctx.s_ve, ctx.s_ac, ctx.s_so
    ld2, gp = ctx.s_ld2, ctx.s_gp
    ten = ctx.ten
    A, B, C = ctx.A, ctx.B, ctx.C
    acc = C.bitcast(F32)
    xnTh = A[:, 0:16384]
    w1T = A[:, 16384:24576]
    w2T = A[:, 24576:32768]
    stg = [B[:, 0:8192], B[:, 8192:16384]]
    hT = B[:, 16384:20480]
    parts = [B[:, 20480:22528].bitcast(F32), B[:, 22528:24576].bitcast(F32)]
    x1t = [B[:, 24576:26624].bitcast(F32), B[:, 26624:28672].bitcast(F32)]
    y1t = [B[:, 28672:29696], B[:, 29696:30720]]
    combT = B[0:8, 30720:32768]
    f32D = ctx.D.bitcast(F32)
    spb2 = ctx.rows[0:8, 0:1024]
    shb2 = ctx.rows[0:1, 1024:2048]
    b1stg = f32D[0:32, (1664 + 2048) // 2:(1664 + 2048) // 2 + 128]

    g.dma(spb2, ten["L1_ffn_spec_b2"], ld2)
    g.dma(shb2, ten["L1_ffn_shared_b2"][None, :], ld2)

    ACTS = [("sh", None)] + [("e%d" % e, e) for e in range(NE)]

    for th in range(2):
        toff = th * 2048
        sy.wait(pe, ctx.s_pe.n)
        for c in range(NC):
            sy.dma(xnTh[:, c * 2048:(c + 1) * 2048].rearrange("p (t f) -> p t f", t=16),
                   ten["s_xnt1"][th * 16:(th + 1) * 16, c].transpose([1, 0, 2]), ld)
        t.wait(ve)
        t.wait(ctx.s_gp)
        for tt in range(16):
            gt = th * 16 + tt
            t.op("transpose", out=ctx.PS[0:8, 2048 + tt * 128:2048 + (tt + 1) * 128],
                 in_=ctx.comb[:, gt * 8:(gt + 1) * 8],
                 identity=ctx.id_f32, inc=(pe if tt == 15 else None))
        v.wait(pe)
        v.op("tensor_copy", inc=ve, out=combT, in_=ctx.PS[0:8, 2048:4096])
        t.wait(ve)
        v.wait(so)
        ve_done = [0, 0]
        sy.wait(ve, ctx.s_ve.n)
        for tt in range(16):
            sl = tt % 2
            gt = th * 16 + tt
            if tt >= 2:
                sy.wait(ve, ve_done[sl])
            sy.dma(x1t[sl], ten["s_x1"][gt], ld)
            sy.dma(y1t[sl], ten["s_y1"][gt], ld)
            v.wait(ld)
            v.op("tensor_tensor", inc=ve, out=acc[:, tt * 1024:(tt + 1) * 1024],
                 in0=x1t[sl], in1=y1t[sl], op=OP.add)
            ve_done[sl] = ctx.s_ve.n

        stg_free = [0, 0]
        part_free = [0, 0]
        for ei, (ename, e) in enumerate(ACTS):
            if e is None:
                w1_dram = ten["L1_ffn_shared_w1"]
                w2_dram = ten["L1_ffn_shared_w2"]
                b1_dram = ten["L1_ffn_shared_b1"]
            else:
                w1_dram = ten["L1_ffn_spec_w1"][e]
                w2_dram = ten["L1_ffn_spec_w2"][e]
                b1_dram = ten["L1_ffn_spec_b1"][e]
            g.wait(pe, stg_free[0])
            g.dma(b1stg, b1_dram.rearrange("(c p) -> c p", p=128), ld2)
            t.wait(ld2)
            t.op("transpose", inc=pe, out=ctx.PS[:, 3584:3616],
                 in_=b1stg, identity=ctx.id_f32[0:32, 0:32])
            v.wait(pe)
            v.op("tensor_copy", inc=ve, out=ctx.eb1T, in_=ctx.PS[:, 3584:3616])
            s.wait(ve)

            for blk in range(4):
                bsl = blk % 2
                f0 = blk * 1024
                g.wait(pe, stg_free[bsl])
                for fs8 in range(8):
                    g.dma(stg[bsl][:, fs8 * 1024:(fs8 + 1) * 1024],
                          w1_dram[f0 + fs8 * 128:f0 + (fs8 + 1) * 128, :], ld2)
                t.wait(ld2)
                for dc in range(NC):
                    for fs8 in range(NC):
                        t.op("transpose",
                             out=ctx.PSbf[:, 6144 + fs8 * 128:6144 + (fs8 + 1) * 128],
                             in_=stg[bsl][:, fs8 * 1024 + dc * 128:fs8 * 1024 + (dc + 1) * 128],
                             identity=ctx.id_bf, inc=(pe if fs8 == NC - 1 else None))
                    v.wait(pe)
                    v.op("tensor_copy", inc=ve, out=w1T[:, dc * 1024:(dc + 1) * 1024],
                         in_=ctx.PSbf[:, 6144:7168])
                    t.wait(ve)
                g.wait(pe, ctx.s_pe.n)
                for dc8 in range(8):
                    g.dma(stg[bsl][:, dc8 * 1024:(dc8 + 1) * 1024],
                          w2_dram[dc8 * 128:(dc8 + 1) * 128, f0:f0 + 1024], ld2)
                t.wait(ld2)
                for fc in range(NC):
                    for dc8 in range(NC):
                        t.op("transpose",
                             out=ctx.PSbf[:, 6144 + dc8 * 128:6144 + (dc8 + 1) * 128],
                             in_=stg[bsl][:, dc8 * 1024 + fc * 128:dc8 * 1024 + (fc + 1) * 128],
                             identity=ctx.id_bf, inc=(pe if dc8 == NC - 1 else None))
                    v.wait(pe)
                    v.op("tensor_copy", inc=ve, out=w2T[:, fc * 1024:(fc + 1) * 1024],
                         in_=ctx.PSbf[:, 6144:7168])
                    t.wait(ve)
                stg_free[bsl] = ctx.s_pe.n

                for tg in range(4):
                    tcol = tg * 512
                    for fs in range(8):
                        hb = fs % 2
                        hps = ctx.PS[:, 2048 + hb * 512:2048 + (hb + 1) * 512]
                        for dc in range(NC):
                            t.op("matmul", out=hps,
                                 lhsT=w1T[:, dc * 1024 + fs * 128:dc * 1024 + (fs + 1) * 128],
                                 rhs=xnTh[:, dc * 2048 + tcol:dc * 2048 + tcol + 512],
                                 start=(dc == 0), stop=(dc == NC - 1),
                                 inc=(pe if dc == NC - 1 else None))
                        s.wait(pe)
                        s.op("activation", inc=ac, out=hT[:, fs * 512:(fs + 1) * 512],
                             in_=hps, func=AF.Gelu_apprx_tanh,
                             bias=ctx.eb1T[:, blk * 8 + fs:blk * 8 + fs + 1])
                        t.wait(ac)
                    for sub in range(4):
                        tt = tg * 4 + sub
                        psl = sub % 2
                        po = ctx.PS[:, psl * 1024:(psl + 1) * 1024]
                        mms = []
                        for fs in range(8):
                            for hh in range(2):
                                mms.append((hT[:, fs * 512 + sub * 128:fs * 512 + sub * 128 + 128],
                                            w2T[:, fs * 1024 + hh * 512:fs * 1024 + (hh + 1) * 512],
                                            po[:, hh * 512:(hh + 1) * 512]))
                        extra = []
                        if e is None and blk == 0:
                            for hh in range(2):
                                extra.append((ctx.ones_row,
                                              shb2[:, hh * 512:(hh + 1) * 512],
                                              po[:, hh * 512:(hh + 1) * 512]))
                                extra.append((combT[:, tt * 128:(tt + 1) * 128],
                                              spb2[:, hh * 512:(hh + 1) * 512],
                                              po[:, hh * 512:(hh + 1) * 512]))
                        allm = mms + extra
                        for i, (lh, rh, oo) in enumerate(allm):
                            t.op("matmul", out=oo, lhsT=lh, rhs=rh,
                                 start=(i < 2), stop=(i >= len(allm) - 2),
                                 inc=(pe if i == len(allm) - 1 else None))
                        s.wait(pe)
                        s.wait(gp, part_free[psl])
                        if e is None:
                            s.op("activation", inc=ac, out=parts[psl], in_=po,
                                 func=AF.Copy)
                        else:
                            gt = th * 16 + tt
                            s.op("activation", inc=ac, out=parts[psl], in_=po,
                                 func=AF.Copy,
                                 scale=ctx.comb[:, gt * 8 + e:gt * 8 + e + 1])
                        t.wait(ac)
                        g.wait(ac)
                        g.op("tensor_tensor", inc=gp,
                             out=acc[:, tt * 1024:(tt + 1) * 1024],
                             in0=parts[psl],
                             in1=acc[:, tt * 1024:(tt + 1) * 1024], op=OP.add)
                        part_free[psl] = ctx.s_gp.n
        outv = ten["out"].rearrange("(th t p) d -> th t p d", th=2, t=16)[th]
        sy.wait(gp)
        sy.dma(outv.transpose([1, 0, 2]),
               acc.rearrange("p (t d) -> p t d", t=16), so)
    sy.wait(so)
    for q in (ctx.qt, ctx.qv, ctx.qs, ctx.qg):
        q.wait(ctx.s_so)


PH_FUNCS = {"ph0": ph0, "ph1": ph1, "ph2": ph2, "ph3": ph3, "ph4": ph4,
            "ph4b": ph4b, "ph5": ph5, "ph6": ph6, "ph7": ph7, "ph8": ph8}

DEBUG_DUMPS = []


def build_group(group_phases):
    nc = bass.Bass("TRN2", target_bir_lowering=False)
    ctx = Ctx()
    ctx.nc = nc
    ctx.ten, input_names = declare_group_tensors(nc, group_phases)
    dump_tens = []
    for nm, shape, dt, fn in DEBUG_DUMPS:
        dump_tens.append((nc.dram_tensor("dbg_" + nm, shape, dt,
                                         kind="ExternalOutput").ap(), fn))
    stack = ExitStack()
    with stack:
        alloc_sbuf(nc, ctx, stack)
        sems = {}
        for nm in ["bar", "ld", "ld2", "pe", "ve", "ac", "so", "gp"]:
            sems[nm] = Sem(stack.enter_context(nc.semaphore(nm)))
        ctx.sem_bar = sems["bar"]
        ctx.s_ld, ctx.s_ld2 = sems["ld"], sems["ld2"]
        ctx.s_pe, ctx.s_ve, ctx.s_ac = sems["pe"], sems["ve"], sems["ac"]
        ctx.s_so, ctx.s_gp = sems["so"], sems["gp"]
        ctx.qt, ctx.qv, ctx.qs = Q("tensor"), Q("vector"), Q("scalar")
        ctx.qg, ctx.qy = Q("gpsimd"), Q("sync")
        ctx.queues = [ctx.qt, ctx.qv, ctx.qs, ctx.qg, ctx.qy]

        for phn in group_phases:
            PH_FUNCS[phn](ctx)
            barrier(ctx)
        for ap, fn in dump_tens:
            ctx.qy.dma(ap, fn(ctx), ctx.s_so)
        ctx.qy.wait(ctx.s_so)

        with nc.Block() as block:
            block.tensor(lambda e: [f(e) for f in ctx.qt.thunks])
            block.vector(lambda e: [f(e) for f in ctx.qv.thunks])
            block.scalar(lambda e: [f(e) for f in ctx.qs.thunks])
            block.gpsimd(lambda e: [f(e) for f in ctx.qg.thunks])
            block.sync(lambda e: [f(e) for f in ctx.qy.thunks])
    return nc, input_names


def run_groups(inputs, groups, n=8):
    x_full = np.ascontiguousarray(inputs["L0_attn_hidden_states"])
    mask = np.where(np.triu(np.ones((128, 128), dtype=bool), 1),
                    np.float32(-1e30), np.float32(0.0)).astype(np.float32)
    const = {"c_ident": np.eye(128, dtype=np.float32), "c_mask": mask}
    weights = {k: np.ascontiguousarray(v, dtype=np.float32)
               for k, v in inputs.items() if k != "L0_attn_hidden_states"}
    carry = [dict() for _ in range(n)]
    for group in groups:
        nc, input_names = build_group(group)
        in_maps = []
        for c in range(n):
            m = {}
            for nm in input_names:
                if nm == "x":
                    m[nm] = np.ascontiguousarray(x_full[c])
                elif nm in const:
                    m[nm] = const[nm]
                elif nm in weights:
                    m[nm] = weights[nm]
                else:
                    m[nm] = carry[c][nm]
            in_maps.append(m)
        res = run_bass_kernel_spmd(nc, in_maps, core_ids=list(range(n)))
        for c in range(n):
            for k, vv in res.results[c].items():
                carry[c][k] = vv
    return carry


def kernel(**inputs):
    carry = run_groups(inputs, GROUPS)
    return np.stack([c["out"] for c in carry])


# version 02
# speedup vs baseline: 1.4415x; 1.4415x over previous
import numpy as np
import ml_dtypes
import concourse.bass as bass
import concourse.mybir as mybir
from concourse.bass_utils import run_bass_kernel_spmd
from contextlib import ExitStack

F32 = mybir.dt.float32
BF16 = mybir.dt.bfloat16
AF = mybir.ActivationFunctionType
OP = mybir.AluOpType
AX = mybir.AxisListType
bf16 = ml_dtypes.bfloat16

T = 4096
D = 1024
DFF = 4096
NH, HD = 16, 64
NT = T // 128
NC = D // 128
NE = 8

SCRATCH = {
    "s_attn0": ([NT, 128, D], BF16),
    "s_ffn0": ([NT, 128, D], BF16),
    "s_x1": ([NT, 128, D], F32),
    "s_xnt1": ([NT, NC, 128, 128], BF16),
    "s_qt": ([NT, NC, 128, 128], BF16),
    "s_y1": ([NT, 128, D], BF16),
}
PHASES = {
    "ph0": (["c_ident", "L0_attn_b_v", "L0_attn_b_o", "L0_ffn_b2",
             "L0_ffn_b1", "L0_ffn_quantum_phases", "L0_ffn_quantum_weights_real",
             "L0_ffn_quantum_weights_imag", "L0_attn_W_basis",
             "L1_ffn_router_w", "L1_ffn_router_b", "L1_attn_ln_g", "L1_attn_ln_b"],
            [], []),
    "ph1": (["x"], [], []),
    "ph2": (["L0_attn_W_v", "L0_attn_W_o"], [], []),
    "ph3": (["c_mask"], [], ["s_attn0"]),
    "ph4": (["L0_ffn_W1", "L0_ffn_W2"], [], ["s_ffn0"]),
    "ph4b": (["x"], ["s_attn0", "s_ffn0"], ["s_x1"]),
    "ph5": (["L1_attn_Wq", "L1_attn_bq", "L1_attn_Wk", "L1_attn_bk",
             "L1_attn_Wv", "L1_attn_bv"], ["s_x1"], ["s_xnt1", "s_qt"]),
    "ph6": ([], ["s_qt"], ["s_y1"]),
    "ph7": ([], [], []),
    "ph8": (["L1_ffn_shared_w1", "L1_ffn_shared_b1", "L1_ffn_shared_w2",
             "L1_ffn_shared_b2", "L1_ffn_spec_w1", "L1_ffn_spec_b1",
             "L1_ffn_spec_w2", "L1_ffn_spec_b2"],
            ["s_x1", "s_y1", "s_xnt1"], []),
}
PHASE_ORDER = ["ph0", "ph1", "ph2", "ph3", "ph4", "ph4b", "ph5", "ph6", "ph7", "ph8"]
GROUPS = [PHASE_ORDER]

INPUT_SPECS = {
    "L0_attn_W_basis": (64, 1024), "L0_attn_W_v": (1024, 1024),
    "L0_attn_b_v": (1024,), "L0_attn_W_o": (1024, 1024), "L0_attn_b_o": (1024,),
    "L0_ffn_W1": (1024, 4096), "L0_ffn_b1": (4096,),
    "L0_ffn_quantum_phases": (4096,), "L0_ffn_quantum_weights_real": (4096,),
    "L0_ffn_quantum_weights_imag": (4096,), "L0_ffn_W2": (4096, 1024),
    "L0_ffn_b2": (1024,), "L1_attn_ln_g": (1024,), "L1_attn_ln_b": (1024,),
    "L1_attn_Wq": (1024, 1024), "L1_attn_bq": (1024,),
    "L1_attn_Wk": (1024, 1024), "L1_attn_bk": (1024,),
    "L1_attn_Wv": (1024, 1024), "L1_attn_bv": (1024,),
    "L1_ffn_shared_w1": (4096, 1024), "L1_ffn_shared_b1": (4096,),
    "L1_ffn_shared_w2": (1024, 4096), "L1_ffn_shared_b2": (1024,),
    "L1_ffn_router_w": (8, 1024), "L1_ffn_router_b": (8,),
    "L1_ffn_spec_w1": (8, 4096, 1024), "L1_ffn_spec_b1": (8, 4096),
    "L1_ffn_spec_w2": (8, 1024, 4096), "L1_ffn_spec_b2": (8, 1024),
    "x": (T, D), "c_ident": (128, 128), "c_mask": (128, 128),
}


class Sem:
    def __init__(self, h):
        self.h = h
        self.n = 0


class Q:

    def __init__(self, name):
        self.name = name
        self.thunks = []

    def raw(self, fn):
        self.thunks.append(fn)

    def wait(self, sem, val=None):
        v = sem.n if val is None else int(val)
        self.raw(lambda e, s=sem.h, v=v: e.wait_ge(s, v))

    def op(self, meth, inc=None, amt=1, **kw):
        if inc is not None:
            inc.n += amt

        def fn(e, meth=meth, kw=kw, inc=inc, amt=amt):
            inst = getattr(e, meth)(**kw)
            if inc is not None:
                inst.then_inc(inc.h, amt)
        self.raw(fn)

    def dma(self, out, in_, inc, **kw):
        inc.n += 16

        def fn(e, out=out, in_=in_, inc=inc, kw=kw):
            e.dma_start(out=out, in_=in_, **kw).then_inc(inc.h, 16)
        self.raw(fn)


class Ctx:
    pass


def declare_group_tensors(nc, group_phases):
    later = []
    idx = max(PHASE_ORDER.index(p) for p in group_phases)
    for p in PHASE_ORDER[idx + 1:]:
        later.append(p)
    later_reads = set()
    for ph in later:
        later_reads |= set(PHASES[ph][1])
    reads, writes, uses = set(), set(), set()
    for ph in group_phases:
        u, r, w = PHASES[ph]
        uses |= set(u)
        reads |= {s for s in r if s not in writes}
        writes |= set(w)
    t = {}
    input_names = []
    for name in sorted(uses):
        t[name] = nc.dram_tensor(name, list(INPUT_SPECS[name]), F32,
                                 kind="ExternalInput").ap()
        input_names.append(name)
    for sname in sorted(reads | writes):
        shape, dt = SCRATCH[sname]
        if sname in reads and sname not in writes:
            kind = "ExternalInput"
            input_names.append(sname)
        elif sname in later_reads:
            kind = "ExternalOutput"
        else:
            kind = "Internal"
        t[sname] = nc.dram_tensor(sname, shape, dt, kind=kind).ap()
    if "ph8" in group_phases:
        t["out"] = nc.dram_tensor("out", [T, D], F32, kind="ExternalOutput").ap()
    return t, input_names


def alloc_sbuf(nc, ctx, stack):
    A = stack.enter_context(nc.sbuf_tensor("arenaA", [128, 32768], BF16))
    B = stack.enter_context(nc.sbuf_tensor("arenaB", [128, 32768], BF16))
    C = stack.enter_context(nc.sbuf_tensor("arenaC", [128, 32768], BF16))
    Dm = stack.enter_context(nc.sbuf_tensor("arenaD", [128, 7936], BF16))
    PS = stack.enter_context(nc.psum_tensor("ps8", [128, 4096], F32))
    ctx.A, ctx.B, ctx.C, ctx.D, ctx.PS = A[:], B[:], C[:], Dm[:], PS[:]
    ctx.PSbf = ctx.PS.bitcast(BF16)
    ctx.id_bf = ctx.D[:, 0:128]
    ctx.id_f32 = ctx.D[:, 128:384].bitcast(F32)
    ctx.ones_row = ctx.D[0:1, 384:512]
    ctx.logits = ctx.D[:, 640:1152].bitcast(F32)
    ctx.comb = ctx.D[:, 1152:1664].bitcast(F32)
    ctx.rows = ctx.D[:, 1664:5760]
    co = 5760

    def colf32(off, n):
        return ctx.D[:, co + off:co + off + 2 * n].bitcast(F32)
    ctx.b1T = colf32(0, 32)
    ctx.sinT = colf32(64, 32)
    ctx.cosT = colf32(128, 32)
    ctx.aT = colf32(192, 32)
    ctx.gT = colf32(256, 8)
    ctx.qbias = colf32(272, 8)
    ctx.kbias = colf32(288, 8)
    ctx.stat = colf32(304, 16)
    ctx.red = colf32(336, 16)
    ctx.tmpc = colf32(368, 64)
    ctx.eb1T = colf32(496, 32)
    ctx.routwT = ctx.D[:, co + 560:co + 624]
    ctx.bT_bf = ctx.D[:, co + 624:co + 632]
    ctx.spare = ctx.D[:, 6784:7936]
    ctx.out0_evac = ctx.spare[:, 0:1024]


def barrier(ctx):
    bar = ctx.sem_bar
    for q in ctx.queues:
        bar.n += 1
        q.raw(lambda e, s=bar.h: e.nop().then_inc(s, 1))
    tgt = bar.n
    for q in ctx.queues:
        q.raw(lambda e, s=bar.h, v=tgt: e.wait_ge(s, v))


def emit_ln(ctx, x_ap, out_bf, sq, eps=1e-5):
    v, s = ctx.qv, ctx.qs
    ve, ac = ctx.s_ve, ctx.s_ac
    st = ctx.stat
    s.wait(ve)
    s.op("activation", inc=ac, out=sq, in_=x_ap, func=AF.Square,
         accum_out=st[:, 0:1], amt=2)
    v.op("tensor_reduce", inc=ve, out=st[:, 1:2], in_=x_ap, axis=AX.X, op=OP.add)
    s.wait(ve)
    s.op("activation", out=st[:, 4:5], in_=st[:, 1:2], func=AF.Square, scale=1.0 / D)
    s.op("activation", inc=ac, out=st[:, 8:9], in_=st[:, 1:2], func=AF.Copy,
         scale=-1.0 / D)
    v.wait(ac)
    v.op("tensor_scalar", inc=ve, out=st[:, 5:6], in0=st[:, 0:1],
         scalar1=1.0 / D, scalar2=st[:, 4:5], op0=OP.mult, op1=OP.subtract)
    s.wait(ve)
    s.op("activation", inc=ac, out=st[:, 6:7], in_=st[:, 5:6], func=AF.Sqrt,
         bias=st[:, 15:16])
    v.wait(ac)
    v.op("reciprocal", inc=ve, out=st[:, 7:8], in_=st[:, 6:7])
    v.wait(ve)
    v.op("tensor_scalar", inc=ve, out=out_bf, in0=x_ap, scalar1=st[:, 8:9],
         scalar2=st[:, 7:8], op0=OP.add, op1=OP.mult)



def ph0(ctx):
    t, v, s, g, sy = ctx.qt, ctx.qv, ctx.qs, ctx.qg, ctx.qy
    ld, pe, ve, ac = ctx.s_ld, ctx.s_pe, ctx.s_ve, ctx.s_ac
    ten = ctx.ten
    C = ctx.C
    f32C = C.bitcast(F32)

    sy.dma(ctx.id_f32, ten["c_ident"], ld)
    g.dma(ctx.id_bf, ten["c_ident"], ld)
    g.dma(ctx.rows[0:1, 0:1024], ten["L0_attn_b_v"][None, :], ld)
    g.dma(ctx.rows[0:1, 1024:2048], ten["L0_attn_b_o"][None, :], ld)
    g.dma(ctx.rows[0:1, 2048:3072], ten["L0_ffn_b2"][None, :], ld)
    g.dma(ctx.spare[0:1, 1024:1032], ten["L1_ffn_router_b"][None, :], ld)
    g.dma(C[0:8, 0:1024], ten["L1_ffn_router_w"], ld)
    g.dma(C[0:64, 1024:2048], ten["L0_attn_W_basis"], ld)
    th_st = f32C[0:32, 1024:1152]
    wr_st = f32C[0:32, 1152:1280]
    wi_st = f32C[0:32, 1280:1408]
    b1_st = f32C[0:32, 1408:1536]
    g_st = f32C[0:8, 1536:1664]
    b_st = f32C[0:8, 1664:1792]
    sy.dma(th_st, ten["L0_ffn_quantum_phases"].rearrange("(c p) -> c p", p=128), ld)
    sy.dma(wr_st, ten["L0_ffn_quantum_weights_real"].rearrange("(c p) -> c p", p=128), ld)
    sy.dma(wi_st, ten["L0_ffn_quantum_weights_imag"].rearrange("(c p) -> c p", p=128), ld)
    sy.dma(b1_st, ten["L0_ffn_b1"].rearrange("(c p) -> c p", p=128), ld)
    sy.dma(g_st, ten["L1_attn_ln_g"].rearrange("(c p) -> c p", p=128), ld)
    sy.dma(b_st, ten["L1_attn_ln_b"].rearrange("(c p) -> c p", p=128), ld)

    v.wait(ld)
    v.op("memset", ap=ctx.ones_row, constant=1.0)
    v.op("memset", inc=ve, ap=ctx.stat[:, 15:16], constant=1e-5)

    def vstep(meth, **kw):
        v.op(meth, inc=ve, **kw)
        v.wait(ve)

    PSf = ctx.PS
    t.wait(ld)
    t.op("transpose", out=PSf[:, 0:32], in_=th_st, identity=ctx.id_f32[0:32, 0:32])
    t.op("transpose", out=PSf[:, 32:64], in_=wr_st, identity=ctx.id_f32[0:32, 0:32])
    t.op("transpose", out=PSf[:, 64:96], in_=wi_st, identity=ctx.id_f32[0:32, 0:32])
    t.op("transpose", out=PSf[:, 96:128], in_=b1_st, identity=ctx.id_f32[0:32, 0:32])
    t.op("transpose", out=PSf[:, 128:136], in_=g_st, identity=ctx.id_f32[0:8, 0:8])
    t.op("transpose", inc=pe, out=PSf[:, 136:144], in_=b_st, identity=ctx.id_f32[0:8, 0:8])
    v.wait(pe)
    thT = ctx.tmpc[:, 0:32]
    wiT = ctx.tmpc[:, 32:64]
    wrT = ctx.tmpc[:, 96:128] if False else ctx.tmpc[:, 0:0]
    wrT = ctx.red[:, 0:16]
    wrT = ctx.aT
    v.op("tensor_copy", out=thT, in_=PSf[:, 0:32])
    v.op("tensor_copy", out=wrT, in_=PSf[:, 32:64])
    v.op("tensor_copy", out=wiT, in_=PSf[:, 64:96])
    v.op("tensor_copy", out=ctx.b1T, in_=PSf[:, 96:128])
    v.op("tensor_copy", out=ctx.gT, in_=PSf[:, 128:136])
    v.op("tensor_copy", inc=ve, out=ctx.bT_bf, in_=PSf[:, 136:144])
    t.wait(ve)
    v.wait(ve)

    twopi = float(2 * np.pi)
    vstep("scalar_tensor_tensor", out=thT, in0=ctx.b1T, scalar=0.1,
          in1=thT, op0=OP.mult, op1=OP.add)
    msk = ctx.sinT
    for it in range(8):
        vstep("tensor_single_scalar", out=msk, in_=thT, scalar=float(np.pi), op=OP.is_ge)
        vstep("scalar_tensor_tensor", out=thT, in0=msk, scalar=-twopi,
              in1=thT, op0=OP.mult, op1=OP.add)
        vstep("tensor_single_scalar", out=msk, in_=thT, scalar=float(-np.pi), op=OP.is_lt)
        vstep("scalar_tensor_tensor", out=thT, in0=msk, scalar=twopi,
              in1=thT, op0=OP.mult, op1=OP.add)
    s.wait(ve)
    s.op("activation", inc=ac, out=ctx.sinT, in_=thT, func=AF.Sin)
    s.wait(ac)
    s.op("activation", inc=ac, out=ctx.cosT, in_=thT, func=AF.Sin, scale=0.5)
    v.wait(ac)
    vstep("tensor_tensor", out=ctx.cosT, in0=ctx.cosT, in1=ctx.cosT, op=OP.mult)
    vstep("tensor_scalar", out=ctx.cosT, in0=ctx.cosT, scalar1=-2.0, scalar2=1.0,
          op0=OP.mult, op1=OP.add)
    qp = thT
    vstep("tensor_tensor", out=qp, in0=ctx.cosT, in1=wrT, op=OP.mult)
    vstep("tensor_tensor", out=wrT, in0=ctx.sinT, in1=wiT, op=OP.mult)
    vstep("tensor_tensor", out=qp, in0=qp, in1=wrT, op=OP.add)
    vstep("tensor_tensor", out=ctx.aT, in0=qp, in1=wiT, op=OP.mult)
    vstep("tensor_single_scalar", out=ctx.aT, in_=ctx.aT, scalar=0.1, op=OP.mult)

    for c in range(NC):
        t.op("transpose", out=ctx.PSbf[:, 2048 + c * 8:2048 + (c + 1) * 8],
             in_=C[0:8, c * 128:(c + 1) * 128],
             identity=ctx.id_bf[0:8, 0:8], inc=(pe if c == NC - 1 else None))
    v.wait(pe)
    v.op("tensor_copy", inc=ve, out=ctx.routwT, in_=ctx.PSbf[:, 2048:2112])
    for c in range(NC):
        t.op("transpose", out=ctx.PSbf[:, 4096 + c * 64:4096 + (c + 1) * 64],
             in_=C[0:64, 1024 + c * 128:1024 + (c + 1) * 128],
             identity=ctx.id_bf[0:64, 0:64], inc=(pe if c == NC - 1 else None))
    v.wait(pe)
    v.op("tensor_copy", inc=ve, out=C[:, 24576:25088], in_=ctx.PSbf[:, 4096:4608])
    t.wait(ve)


def ph1(ctx):
    t, v, s, g, sy = ctx.qt, ctx.qv, ctx.qs, ctx.qg, ctx.qy
    ld, pe, ve, ac = ctx.s_ld, ctx.s_pe, ctx.s_ve, ctx.s_ac
    ten = ctx.ten
    A, C = ctx.A, ctx.C
    f32C = C.bitcast(F32)
    sq = f32C[:, 0:1024]
    xr = [f32C[:, 1024:2048], f32C[:, 2048:3072]]
    xn_bf = C[:, 6144:7168]
    ve_done = [0, 0]

    for tt in range(NT):
        sl = tt % 2
        if tt >= 2:
            sy.wait(ctx.s_ve, ve_done[sl])
        sy.dma(xr[sl], ten["x"][tt * 128:(tt + 1) * 128, :], ld)
        s.wait(ld)
        v.wait(ld)
        emit_ln(ctx, xr[sl], xn_bf, sq)
        ve_done[sl] = ctx.s_ve.n
        t.wait(ve)
        for c in range(NC):
            t.op("transpose", out=ctx.PSbf[:, c * 128:(c + 1) * 128],
                 in_=xn_bf[:, c * 128:(c + 1) * 128],
                 identity=ctx.id_bf, inc=(pe if c == NC - 1 else None))
        v.wait(pe)
        for c in range(NC):
            v.op("tensor_copy", out=A[:, c * 4096 + tt * 128:c * 4096 + (tt + 1) * 128],
                 in_=ctx.PSbf[:, c * 128:(c + 1) * 128],
                 inc=(ve if c == NC - 1 else None))
        t.wait(ve)


def _load_and_transpose_1024(ctx, wname, stg, dest):
    t, v, g = ctx.qt, ctx.qv, ctx.qg
    ld, pe, ve = ctx.s_ld, ctx.s_pe, ctx.s_ve
    g.wait(pe, getattr(ctx, "stg_free_pe", 0))
    for c in range(NC):
        g.dma(stg[:, c * 1024:(c + 1) * 1024],
              ctx.ten[wname][c * 128:(c + 1) * 128, :], ld)
    t.wait(ld)
    for dc in range(NC):
        for oc in range(NC):
            t.op("transpose", out=ctx.PSbf[:, oc * 128:(oc + 1) * 128],
                 in_=stg[:, oc * 1024 + dc * 128:oc * 1024 + (dc + 1) * 128],
                 identity=ctx.id_bf, inc=(pe if oc == NC - 1 else None))
        v.wait(pe)
        v.op("tensor_copy", inc=ve, out=dest[:, dc * 1024:(dc + 1) * 1024],
             in_=ctx.PSbf[:, 0:1024])
        t.wait(ve)
    ctx.stg_free_pe = ctx.s_pe.n


def ph2(ctx):
    t, v, s, g, sy = ctx.qt, ctx.qv, ctx.qs, ctx.qg, ctx.qy
    ld, pe, ve, ac = ctx.s_ld, ctx.s_pe, ctx.s_ve, ctx.s_ac
    ten = ctx.ten
    A, B, C = ctx.A, ctx.B, ctx.C
    stg = C[:, 0:8192]
    WvT = C[:, 8192:16384]
    WoT = C[:, 16384:24576]
    WbT = C[:, 24576:25088]
    latT = C[:, 25088:29184]
    r_bv = ctx.rows[0:1, 0:1024]

    _load_and_transpose_1024(ctx, "L0_attn_W_v", stg, WvT)

    for tt in range(NT):
        for dc in range(NC):
            for h in range(2):
                t.op("matmul", out=ctx.PS[:, h * 512:(h + 1) * 512],
                     lhsT=A[:, dc * 4096 + tt * 128:dc * 4096 + (tt + 1) * 128],
                     rhs=WvT[:, dc * 1024 + h * 512:dc * 1024 + (h + 1) * 512],
                     start=(dc == 0), stop=False)
        for h in range(2):
            t.op("matmul", out=ctx.PS[:, h * 512:(h + 1) * 512],
                 lhsT=ctx.ones_row, rhs=r_bv[:, h * 512:(h + 1) * 512],
                 start=False, stop=True, inc=(pe if h == 1 else None))
        s.wait(pe)
        s.op("activation", inc=ac, out=B[:, tt * 1024:(tt + 1) * 1024],
             in_=ctx.PS[:, 0:1024], func=AF.Copy)
        t.wait(ac)

    _load_and_transpose_1024(ctx, "L0_attn_W_o", stg, WoT)

    for tg in range(8):
        for dc in range(NC):
            t.op("matmul", out=ctx.PS[0:64, 0:512],
                 lhsT=WbT[:, dc * 64:(dc + 1) * 64],
                 rhs=A[:, dc * 4096 + tg * 512:dc * 4096 + (tg + 1) * 512],
                 start=(dc == 0), stop=(dc == NC - 1),
                 inc=(pe if dc == NC - 1 else None))
        v.wait(pe)
        v.op("tensor_copy", inc=ve, out=latT[0:64, tg * 512:(tg + 1) * 512],
             in_=ctx.PS[0:64, 0:512])
        t.wait(ve)


def ph3(ctx):
    t, v, s, g, sy = ctx.qt, ctx.qv, ctx.qs, ctx.qg, ctx.qy
    ld, pe, ve, ac, so = ctx.s_ld, ctx.s_pe, ctx.s_ve, ctx.s_ac, ctx.s_so
    ten = ctx.ten
    A, B, C = ctx.A, ctx.B, ctx.C
    f32C = C.bitcast(F32)
    maskbig = f32C[:, 0:128]
    att = C[:, 8192 + 0:8192 + 4096] if False else C[:, 0:0]
    att = C[:, 4096:8192]
    attT = C[:, 29184 + 0:29184] if False else None
    attT = C[:, 31232 + 0:31232] if False else None
    attT = C[:, 8192:12288]
    WoT = C[:, 16384:24576]
    latT = C[:, 25088:29184]
    y_bf = C[:, 29184:30208]
    yT = C[:, 30208:31232]
    out0 = ctx.out0_evac
    r_bo = ctx.rows[0:1, 1024:2048]
    st_done = 0

    sy.dma(maskbig, ten["c_mask"], ld)
    v.wait(ld)

    for qi in range(NT):
        nk = qi + 1
        klen = nk * 128
        nch = (klen + 511) // 512
        for c in range(nch):
            cols = min(512, klen - c * 512)
            t.op("matmul", out=ctx.PS[:, c * 512:c * 512 + cols],
                 lhsT=latT[0:64, qi * 128:(qi + 1) * 128],
                 rhs=latT[0:64, c * 512:c * 512 + cols],
                 start=True, stop=True, inc=(pe if c == nch - 1 else None))
        v.wait(pe)
        dg = ctx.PS[:, qi * 128:klen]
        v.op("tensor_tensor", inc=ve, out=dg, in0=dg, in1=maskbig, op=OP.add)
        v.wait(ve)
        v.op("tensor_reduce", inc=ve, out=ctx.red[:, 0:1], in_=ctx.PS[:, 0:klen],
             axis=AX.X, op=OP.max, negate=True)
        v.wait(ve)
        v.op("tensor_single_scalar", inc=ve, out=ctx.red[:, 0:1],
             in_=ctx.red[:, 0:1], scalar=0.125, op=OP.mult)
        s.wait(ve)
        s.op("activation", inc=ac, out=att[:, 0:klen], in_=ctx.PS[:, 0:klen],
             func=AF.Exp, scale=0.125, bias=ctx.red[:, 0:1],
             accum_out=ctx.red[:, 1:2], amt=2)
        v.wait(ac)
        v.op("reciprocal", inc=ve, out=ctx.red[:, 2:3], in_=ctx.red[:, 1:2])
        t.wait(ac)
        for kc in range(nk):
            bank = kc // 8
            slot = kc % 8
            t.op("transpose",
                 out=ctx.PSbf[:, bank * 1024 + slot * 128:bank * 1024 + (slot + 1) * 128],
                 in_=att[:, kc * 128:(kc + 1) * 128], identity=ctx.id_bf,
                 inc=(pe if kc == nk - 1 else None))
        nb = (nk + 7) // 8
        v.wait(pe)
        for b in range(nb):
            cols = min(1024, klen - b * 1024)
            v.op("tensor_copy", out=attT[:, b * 1024:b * 1024 + cols],
                 in_=ctx.PSbf[:, b * 1024:b * 1024 + cols],
                 inc=(ve if b == nb - 1 else None))
        t.wait(ve)
        for kc in range(nk):
            for h in range(2):
                t.op("matmul", out=ctx.PS[:, 2048 + h * 512:2048 + (h + 1) * 512],
                     lhsT=attT[:, kc * 128:(kc + 1) * 128],
                     rhs=B[:, kc * 1024 + h * 512:kc * 1024 + (h + 1) * 512],
                     start=(kc == 0), stop=(kc == nk - 1),
                     inc=(pe if (kc == nk - 1 and h == 1) else None))
        s.wait(pe)
        s.wait(ve)
        s.op("activation", inc=ac, out=y_bf, in_=ctx.PS[:, 2048:3072],
             func=AF.Copy, scale=ctx.red[:, 2:3])
        t.wait(ac)
        for c in range(NC):
            t.op("transpose", out=ctx.PSbf[:, c * 128:(c + 1) * 128],
                 in_=y_bf[:, c * 128:(c + 1) * 128], identity=ctx.id_bf,
                 inc=(pe if c == NC - 1 else None))
        v.wait(pe)
        v.op("tensor_copy", inc=ve, out=yT, in_=ctx.PSbf[:, 0:1024])
        t.wait(ve)
        for c in range(NC):
            for h in range(2):
                t.op("matmul", out=ctx.PS[:, 2048 + h * 512:2048 + (h + 1) * 512],
                     lhsT=yT[:, c * 128:(c + 1) * 128],
                     rhs=WoT[:, c * 1024 + h * 512:c * 1024 + (h + 1) * 512],
                     start=(c == 0), stop=False)
        for h in range(2):
            t.op("matmul", out=ctx.PS[:, 2048 + h * 512:2048 + (h + 1) * 512],
                 lhsT=ctx.ones_row, rhs=r_bo[:, h * 512:(h + 1) * 512],
                 start=False, stop=True, inc=(pe if h == 1 else None))
        if qi >= 1:
            s.wait(so, st_done)
        s.wait(pe)
        s.op("activation", inc=ac, out=out0, in_=ctx.PS[:, 2048:3072], func=AF.Copy)
        g.wait(ac)
        g.dma(ten["s_attn0"][qi], out0, so)
        st_done = ctx.s_so.n
        t.wait(ac)
    g.wait(so)


def ph4(ctx):
    t, v, s, g, sy = ctx.qt, ctx.qv, ctx.qs, ctx.qg, ctx.qy
    ld, pe, ve, ac, so = ctx.s_ld, ctx.s_pe, ctx.s_ve, ctx.s_ac, ctx.s_so
    ten = ctx.ten
    A, B, C = ctx.A, ctx.B, ctx.C
    for c in range(NC):
        g.dma(B[:, c * 4096:(c + 1) * 4096],
              ten["L0_ffn_W1"][c * 128:(c + 1) * 128, :], ld)
    for fc in range(32):
        g.dma(C[:, fc * 1024:(fc + 1) * 1024],
              ten["L0_ffn_W2"][fc * 128:(fc + 1) * 128, :], ld)
    t.wait(ld)

    halfpi = ctx.red[:, 8:9]
    v.op("memset", inc=ve, ap=halfpi, constant=float(np.pi / 2))
    s.wait(ve)
    actT = [ctx.spare[:, 0:256], ctx.spare[:, 256:512]]
    snbuf = ctx.spare[:, 512:1024].bitcast(F32)
    csbuf = ctx.D[:, 6784 + 1024:6784 + 1152].bitcast(F32)
    csbuf = ctx.rows[:, 0:512].bitcast(F32)
    outev = ctx.rows[:, 3072 + 0:3072 + 1024]
    r_b2 = ctx.rows[0:1, 2048:3072]

    for tg in range(16):
        tcol = tg * 256
        for fs in range(32):
            hb = fs % 2
            hps = ctx.PS[:, hb * 512:hb * 512 + 256]
            for dc in range(NC):
                t.op("matmul", out=hps,
                     lhsT=B[:, dc * 4096 + fs * 128:dc * 4096 + (fs + 1) * 128],
                     rhs=A[:, dc * 4096 + tcol:dc * 4096 + tcol + 256],
                     start=(dc == 0), stop=(dc == NC - 1),
                     inc=(pe if dc == NC - 1 else None))
            sincol = ctx.sinT[:, fs:fs + 1]
            coscol = ctx.cosT[:, fs:fs + 1]
            acol = ctx.aT[:, fs:fs + 1]
            b1col = ctx.b1T[:, fs:fs + 1]
            s.wait(pe)
            s.op("activation", out=snbuf, in_=hps, func=AF.Sin, scale=0.1)
            s.op("activation", inc=ac, out=csbuf, in_=hps, func=AF.Sin,
                 scale=-0.1, bias=halfpi)
            v.wait(ac)
            v.op("tensor_scalar", out=snbuf, in0=snbuf, scalar1=coscol,
                 scalar2=None, op0=OP.mult)
            v.op("scalar_tensor_tensor", out=snbuf, in0=csbuf, scalar=sincol,
                 in1=snbuf, op0=OP.mult, op1=OP.add)
            v.op("tensor_scalar", out=snbuf, in0=snbuf, scalar1=acol,
                 scalar2=None, op0=OP.mult)
            v.op("tensor_tensor", inc=ve, out=snbuf, in0=snbuf, in1=hps, op=OP.add)
            s.wait(ve)
            s.op("activation", inc=ac, out=actT[hb], in_=snbuf, func=AF.Relu,
                 bias=b1col)
            t.wait(ac)
            for sub in range(2):
                for h in range(2):
                    t.op("matmul",
                         out=ctx.PS[:, 2048 + sub * 1024 + h * 512:2048 + sub * 1024 + (h + 1) * 512],
                         lhsT=actT[hb][:, sub * 128:(sub + 1) * 128],
                         rhs=C[:, fs * 1024 + h * 512:fs * 1024 + (h + 1) * 512],
                         start=(fs == 0), stop=False)
        for sub in range(2):
            for h in range(2):
                t.op("matmul",
                     out=ctx.PS[:, 2048 + sub * 1024 + h * 512:2048 + sub * 1024 + (h + 1) * 512],
                     lhsT=ctx.ones_row, rhs=r_b2[:, h * 512:(h + 1) * 512],
                     start=False, stop=True,
                     inc=(pe if (sub == 1 and h == 1) else None))
        s.wait(pe)
        for sub in range(2):
            if tg > 0 or sub > 0:
                s.wait(so)
            s.op("activation", inc=ac, out=outev,
                 in_=ctx.PS[:, 2048 + sub * 1024:3072 + sub * 1024], func=AF.Copy)
            g.wait(ac)
            g.dma(ten["s_ffn0"][tg * 2 + sub], outev, so)
        t.wait(ac)
    g.wait(so)


def ph4b(ctx):
    t, v, s, g, sy = ctx.qt, ctx.qv, ctx.qs, ctx.qg, ctx.qy
    ld, ve, so = ctx.s_ld, ctx.s_ve, ctx.s_so
    ten = ctx.ten
    B = ctx.B
    f32B = B.bitcast(F32)
    xr = [f32B[:, 0:1024], f32B[:, 1024:2048]]
    ar = [B[:, 4096:5120], B[:, 5120:6144]]
    fr = [B[:, 6144:7168], B[:, 7168:8192]]
    x1r = [f32B[:, 4096:5120], f32B[:, 5120:6144]]
    ve_done = [0, 0]
    so_done = [0, 0]
    for tt in range(NT):
        sl = tt % 2
        if tt >= 2:
            sy.wait(ve, ve_done[sl])
        sy.dma(xr[sl], ten["x"][tt * 128:(tt + 1) * 128, :], ld)
        sy.dma(ar[sl], ten["s_attn0"][tt], ld)
        sy.dma(fr[sl], ten["s_ffn0"][tt], ld)
        v.wait(ld)
        if tt >= 2:
            v.wait(ctx.s_so, so_done[sl])
        v.op("tensor_tensor", out=xr[sl], in0=xr[sl], in1=ar[sl], op=OP.add)
        v.op("tensor_tensor", inc=ve, out=x1r[sl], in0=xr[sl], in1=fr[sl], op=OP.add)
        ve_done[sl] = ctx.s_ve.n
        g.wait(ve)
        g.dma(ten["s_x1"][tt], x1r[sl], ctx.s_so)
        so_done[sl] = ctx.s_so.n
    g.wait(ctx.s_so)


def ph5(ctx):
    t, v, s, g, sy = ctx.qt, ctx.qv, ctx.qs, ctx.qg, ctx.qy
    ld, pe, ve, ac, so = ctx.s_ld, ctx.s_pe, ctx.s_ve, ctx.s_ac, ctx.s_so
    ten = ctx.ten
    A, B, C = ctx.A, ctx.B, ctx.C
    Wq = C[:, 0:8192]
    Wk = C[:, 8192:16384]
    Wv = C[:, 16384:24576]
    xn_bf = C[:, 24576:25600]
    xnc_bf = C[:, 25600:26624]
    xncT = C[:, 26624:27648]
    xnT_t = C[:, 27648:28672]
    QT_t = C[:, 28672:29696]
    sq = C[:, 29696:31744].bitcast(F32)
    bvp_row = ctx.spare[0:1, 0:1024]
    x1r = [ctx.rows[:, 0:2048].bitcast(F32), ctx.rows[:, 2048:4096].bitcast(F32)]

    for nm, dst in (("L1_attn_Wq", Wq), ("L1_attn_Wk", Wk), ("L1_attn_Wv", Wv)):
        for c in range(NC):
            g.dma(dst[:, c * 1024:(c + 1) * 1024],
                  ten[nm][c * 128:(c + 1) * 128, :], ld)
    t.wait(ld)

    brow = C[0:1, 31744:32768]
    names = [("L1_attn_bq", Wq, "q"), ("L1_attn_bk", Wk, "k"),
             ("L1_attn_bv", Wv, "v")]
    for bn, W, which in names:
        g.wait(pe, ctx.s_pe.n)
        g.dma(brow, ten[bn][None, :], ld)
        t.wait(ld)
        for c in range(NC):
            for h in range(2):
                t.op("matmul", out=ctx.PS[0:1, h * 512:(h + 1) * 512],
                     lhsT=ctx.bT_bf[:, c:c + 1],
                     rhs=W[:, c * 1024 + h * 512:c * 1024 + (h + 1) * 512],
                     start=(c == 0), stop=False)
        for h in range(2):
            t.op("matmul", out=ctx.PS[0:1, h * 512:(h + 1) * 512],
                 lhsT=ctx.ones_row[0:1, 0:1], rhs=brow[:, h * 512:(h + 1) * 512],
                 start=False, stop=True, inc=(pe if h == 1 else None))
        if which == "v":
            s.wait(pe)
            s.op("activation", inc=ac, out=bvp_row, in_=ctx.PS[0:1, 0:1024],
                 func=AF.Copy)
            t.wait(ac)
        else:
            v.wait(pe)
            v.op("tensor_copy", inc=ve, out=sq[0:1, 0:1024], in_=ctx.PS[0:1, 0:1024])
            t.wait(ve)
            for c in range(NC):
                t.op("transpose", out=ctx.PS[:, c * 512:c * 512 + 1],
                     in_=sq[0:1, c * 128:(c + 1) * 128],
                     identity=ctx.id_f32[0:1, 0:1],
                     inc=(pe if c == NC - 1 else None))
            v.wait(pe)
            dstcol = ctx.qbias if which == "q" else ctx.kbias
            for c in range(NC):
                v.op("tensor_copy", out=dstcol[:, c:c + 1],
                     in_=ctx.PS[:, c * 512:c * 512 + 1],
                     inc=(ve if c == NC - 1 else None))
            if which == "q":
                v.wait(ve)
                v.op("tensor_single_scalar", inc=ve, out=dstcol, in_=dstcol,
                     scalar=0.125, op=OP.mult)
            t.wait(ve)
    v.wait(ac)
    for W in (Wq, Wk, Wv):
        for c in range(NC):
            v.op("tensor_scalar", out=W[:, c * 1024:(c + 1) * 1024],
                 in0=W[:, c * 1024:(c + 1) * 1024],
                 scalar1=ctx.gT[:, c:c + 1], scalar2=None, op0=OP.mult,
                 inc=(ve if c == NC - 1 else None))
    t.wait(ve)

    ve_done = [0, 0]
    for tt in range(NT):
        sl = tt % 2
        if tt >= 2:
            sy.wait(ve, ve_done[sl])
        sy.dma(x1r[sl], ten["s_x1"][tt], ld)
        s.wait(ld)
        v.wait(ld)
        emit_ln(ctx, x1r[sl], xn_bf, sq)
        ve_done[sl] = ctx.s_ve.n
        s.wait(ve)
        emit_ln(ctx, xn_bf, xnc_bf, sq)
        t.wait(ve)
        for c in range(NC):
            t.op("transpose", out=ctx.PSbf[:, c * 128:(c + 1) * 128],
                 in_=xn_bf[:, c * 128:(c + 1) * 128], identity=ctx.id_bf)
        for c in range(NC):
            t.op("transpose", out=ctx.PSbf[:, 1024 + c * 128:1024 + (c + 1) * 128],
                 in_=xnc_bf[:, c * 128:(c + 1) * 128], identity=ctx.id_bf,
                 inc=(pe if c == NC - 1 else None))
        v.wait(pe)
        v.op("tensor_copy", out=xnT_t, in_=ctx.PSbf[:, 0:1024])
        v.op("tensor_copy", inc=ve, out=xncT, in_=ctx.PSbf[:, 1024:2048])
        g.wait(ve)
        g.dma(ten["s_xnt1"][tt].rearrange("c p f -> p c f"),
              xnT_t.rearrange("p (c f) -> p c f", c=NC), so)
        t.wait(ve)
        for c in range(NC):
            t.op("matmul", out=ctx.PS[:, 2048:2056],
                 lhsT=xnT_t[:, c * 128:(c + 1) * 128],
                 rhs=ctx.routwT[:, c * 8:(c + 1) * 8],
                 start=(c == 0), stop=False)
        t.op("matmul", out=ctx.PS[:, 2048:2056], lhsT=ctx.ones_row,
             rhs=ctx.spare[0:1, 1024:1032], start=False, stop=True, inc=pe)
        v.wait(pe)
        v.op("tensor_copy", inc=ve, out=ctx.logits[:, tt * 8:(tt + 1) * 8],
             in_=ctx.PS[:, 2048:2056])
        for W, bias, is_q in ((Wq, ctx.qbias, True), (Wk, ctx.kbias, False)):
            for oc in range(NC):
                pso = ctx.PS[:, 2560 + (oc % 2) * 128:2560 + (oc % 2) * 128 + 128]
                for c in range(NC):
                    t.op("matmul", out=pso,
                         lhsT=W[:, c * 1024 + oc * 128:c * 1024 + (oc + 1) * 128],
                         rhs=xncT[:, c * 128:(c + 1) * 128],
                         start=(c == 0), stop=(c == NC - 1),
                         inc=(pe if c == NC - 1 else None))
                s.wait(pe)
                if is_q:
                    s.op("activation", inc=ac, out=QT_t[:, oc * 128:(oc + 1) * 128],
                         in_=pso, func=AF.Identity, scale=0.125, bias=bias[:, oc:oc + 1])
                else:
                    s.op("activation", inc=ac,
                         out=A[:, oc * 4096 + tt * 128:oc * 4096 + (tt + 1) * 128],
                         in_=pso, func=AF.Identity, bias=bias[:, oc:oc + 1])
                t.wait(ac)
            if is_q:
                g.wait(ac)
                g.dma(ten["s_qt"][tt].rearrange("c p f -> p c f"),
                      QT_t.rearrange("p (c f) -> p c f", c=NC), so)
        for c in range(NC):
            for h in range(2):
                t.op("matmul", out=ctx.PS[:, h * 512:(h + 1) * 512],
                     lhsT=xncT[:, c * 128:(c + 1) * 128],
                     rhs=Wv[:, c * 1024 + h * 512:c * 1024 + (h + 1) * 512],
                     start=(c == 0), stop=False)
        for h in range(2):
            t.op("matmul", out=ctx.PS[:, h * 512:(h + 1) * 512],
                 lhsT=ctx.ones_row, rhs=bvp_row[:, h * 512:(h + 1) * 512],
                 start=False, stop=True, inc=(pe if h == 1 else None))
        s.wait(pe)
        s.op("activation", inc=ac, out=B[:, tt * 1024:(tt + 1) * 1024],
             in_=ctx.PS[:, 0:1024], func=AF.Copy)
        t.wait(ac)
    g.wait(so)


def ph6(ctx):
    t, v, s, g, sy = ctx.qt, ctx.qv, ctx.qs, ctx.qg, ctx.qy
    ld, pe, ve, ac, so = ctx.s_ld, ctx.s_pe, ctx.s_ve, ctx.s_ac, ctx.s_so
    ten = ctx.ten
    A, B, C = ctx.A, ctx.B, ctx.C
    QTr = [C[:, 0:1024], C[:, 1024:2048]]
    y_t = C[:, 2048:3072]
    att = C[:, 4096:8192]
    attT = C[:, 8192:12288]
    pe_done = [0, 0]
    st_done = 0
    for qi in range(NT):
        sl = qi % 2
        if qi >= 2:
            sy.wait(pe, pe_done[sl])
        sy.dma(QTr[sl], ten["s_qt"][qi].rearrange("c p f -> p c f"), ld)
        t.wait(ld)
        for h in range(NH):
            c = h // 2
            half = (h % 2) * 64
            for kc in range(8):
                t.op("matmul", out=ctx.PS[:, kc * 512:(kc + 1) * 512],
                     lhsT=QTr[sl][half:half + 64, c * 128:(c + 1) * 128],
                     rhs=A[half:half + 64, c * 4096 + kc * 512:c * 4096 + (kc + 1) * 512],
                     start=True, stop=True, inc=(pe if kc == 7 else None))
            if h == NH - 1:
                pe_done[sl] = ctx.s_pe.n
            v.wait(pe)
            v.op("tensor_reduce", inc=ve, out=ctx.red[:, 0:1], in_=ctx.PS,
                 axis=AX.X, op=OP.max, negate=True)
            s.wait(ve)
            s.op("activation", inc=ac, out=att, in_=ctx.PS, func=AF.Exp,
                 bias=ctx.red[:, 0:1], accum_out=ctx.red[:, 1:2], amt=2)
            v.wait(ac)
            v.op("reciprocal", inc=ve, out=ctx.red[:, 2:3], in_=ctx.red[:, 1:2])
            t.wait(ac)
            for kc in range(32):
                bank = kc // 8
                slot = kc % 8
                t.op("transpose",
                     out=ctx.PSbf[:, bank * 1024 + slot * 128:bank * 1024 + (slot + 1) * 128],
                     in_=att[:, kc * 128:(kc + 1) * 128], identity=ctx.id_bf,
                     inc=(pe if kc == 31 else None))
            v.wait(pe)
            for b in range(4):
                v.op("tensor_copy", out=attT[:, b * 1024:(b + 1) * 1024],
                     in_=ctx.PSbf[:, b * 1024:(b + 1) * 1024],
                     inc=(ve if b == 3 else None))
            t.wait(ve)
            for kc in range(32):
                t.op("matmul", out=ctx.PS[:, 2048:2112],
                     lhsT=attT[:, kc * 128:(kc + 1) * 128],
                     rhs=B[:, kc * 1024 + h * 64:kc * 1024 + (h + 1) * 64],
                     start=(kc == 0), stop=(kc == 31),
                     inc=(pe if kc == 31 else None))
            s.wait(pe)
            s.wait(ve)
            if h == 0 and qi >= 1:
                s.wait(so, st_done)
            s.op("activation", inc=ac, out=y_t[:, h * 64:(h + 1) * 64],
                 in_=ctx.PS[:, 2048:2112], func=AF.Copy, scale=ctx.red[:, 2:3])
            t.wait(ac)
        g.wait(ac)
        g.dma(ten["s_y1"][qi], y_t, so)
        st_done = ctx.s_so.n
    g.wait(so)


def ph7(ctx):
    t, v, s, g = ctx.qt, ctx.qv, ctx.qs, ctx.qg
    pe, ve, ac = ctx.s_pe, ctx.s_ve, ctx.s_ac
    lg = ctx.logits
    cb = ctx.comb
    l3 = lg.rearrange("p (t e) -> p t e", e=8)
    c3 = cb.rearrange("p (t e) -> p t e", e=8)
    m1 = ctx.tmpc[:, 0:32]
    m2 = ctx.tmpc[:, 32:64]
    sp = ctx.spare[:, 0:256].bitcast(F32)
    den = sp[:, 0:32]
    w1n = sp[:, 32:64]
    w2n = sp[:, 64:96]
    eq1 = sp[:, 96:128]

    ve = ctx.s_ve

    def vstep(meth, **kw):
        v.op(meth, inc=ve, **kw)
        v.wait(ve)

    s.op("activation", inc=ac, out=lg, in_=lg, func=AF.Exp)
    v.wait(ac)
    vstep("tensor_reduce", out=m2.rearrange("p (t o) -> p t o", o=1),
          in_=l3, axis=AX.X, op=OP.add)
    vstep("reciprocal", out=m2, in_=m2)
    for e in range(8):
        v.op("tensor_tensor", inc=(ve if e == 7 else None),
             out=l3[:, :, e], in0=l3[:, :, e], in1=m2, op=OP.mult)
    v.wait(ve)
    vstep("tensor_reduce", out=m1.rearrange("p (t o) -> p t o", o=1),
          in_=l3, axis=AX.X, op=OP.max)
    for e in range(8):
        vstep("tensor_tensor", out=c3[:, :, e], in0=l3[:, :, e], in1=m1, op=OP.is_equal)
        vstep("scalar_tensor_tensor", out=c3[:, :, e], in0=c3[:, :, e],
              scalar=-1e9, in1=l3[:, :, e], op0=OP.mult, op1=OP.add)
    vstep("tensor_reduce", out=m2.rearrange("p (t o) -> p t o", o=1),
          in_=c3, axis=AX.X, op=OP.max)
    vstep("tensor_tensor", out=den, in0=m1, in1=m2, op=OP.add)
    vstep("reciprocal", out=den, in_=den)
    vstep("tensor_tensor", out=w1n, in0=m1, in1=den, op=OP.mult)
    vstep("tensor_tensor", out=w2n, in0=m2, in1=den, op=OP.mult)
    for e in range(8):
        vstep("tensor_tensor", out=eq1, in0=l3[:, :, e], in1=m1, op=OP.is_equal)
        vstep("tensor_tensor", out=eq1, in0=eq1, in1=w1n, op=OP.mult)
        vstep("tensor_tensor", out=c3[:, :, e], in0=c3[:, :, e], in1=m2, op=OP.is_equal)
        vstep("tensor_tensor", out=c3[:, :, e], in0=c3[:, :, e], in1=w2n, op=OP.mult)
        vstep("tensor_tensor", out=c3[:, :, e], in0=c3[:, :, e], in1=eq1, op=OP.add)


def ph8(ctx):
    t, v, s, g, sy = ctx.qt, ctx.qv, ctx.qs, ctx.qg, ctx.qy
    ld, pe, ve, ac, so = ctx.s_ld, ctx.s_pe, ctx.s_ve, ctx.s_ac, ctx.s_so
    ld2, gp = ctx.s_ld2, ctx.s_gp
    ten = ctx.ten
    A, B, C = ctx.A, ctx.B, ctx.C
    acc = C.bitcast(F32)
    xnTh = A[:, 0:16384]
    w1T = A[:, 16384:24576]
    w2T = A[:, 24576:32768]
    stg1 = B[:, 0:8192]
    stg2 = B[:, 8192:16384]
    hT = B[:, 16384:20480]
    parts = [B[:, 20480:22528].bitcast(F32), B[:, 22528:24576].bitcast(F32)]
    x1t = [B[:, 24576:26624].bitcast(F32), B[:, 26624:28672].bitcast(F32)]
    y1t = [B[:, 28672:29696], B[:, 29696:30720]]
    combT = B[0:8, 30720:32768]
    f32D = ctx.D.bitcast(F32)
    spb2 = ctx.rows[0:8, 0:1024]
    shb2 = ctx.rows[0:1, 1024:2048]
    b1stg = [f32D[0:32, 1856:1984], f32D[0:32, 1984:2112]]
    T0 = ctx.PSbf[:, 6144:7168]
    T1 = ctx.PSbf[:, 7168:8192]

    g.dma(spb2, ten["L1_ffn_spec_b2"], ld2)
    g.dma(shb2, ten["L1_ffn_shared_b2"][None, :], ld2)

    ACTS = [("sh", None)] + [("e%d" % e, e) for e in range(NE)]

    def drams(e):
        if e is None:
            return (ten["L1_ffn_shared_w1"], ten["L1_ffn_shared_w2"],
                    ten["L1_ffn_shared_b1"])
        return (ten["L1_ffn_spec_w1"][e], ten["L1_ffn_spec_w2"][e],
                ten["L1_ffn_spec_b1"][e])

    for th in range(2):
        jobs = [(e, blk) for _, e in ACTS for blk in range(4)]
        nj = len(jobs)
        sy.wait(pe, ctx.s_pe.n)
        for c in range(NC):
            sy.dma(xnTh[:, c * 2048:(c + 1) * 2048].rearrange("p (t f) -> p t f", t=16),
                   ten["s_xnt1"][th * 16:(th + 1) * 16, c].transpose([1, 0, 2]), ld)
        t.wait(ve)
        t.wait(gp)
        for tt in range(16):
            gt = th * 16 + tt
            t.op("transpose", out=ctx.PS[0:8, 2048 + tt * 128:2048 + (tt + 1) * 128],
                 in_=ctx.comb[:, gt * 8:(gt + 1) * 8],
                 identity=ctx.id_f32, inc=(pe if tt == 15 else None))
        v.wait(pe)
        v.op("tensor_copy", inc=ve, out=combT, in_=ctx.PS[0:8, 2048:4096])
        t.wait(ve)
        v.wait(so)
        ve_done = [0, 0]
        sy.wait(ve, ctx.s_ve.n)
        for tt in range(16):
            sl = tt % 2
            gt = th * 16 + tt
            if tt >= 2:
                sy.wait(ve, ve_done[sl])
            sy.dma(x1t[sl], ten["s_x1"][gt], ld)
            sy.dma(y1t[sl], ten["s_y1"][gt], ld)
            v.wait(ld)
            v.op("tensor_tensor", inc=ve, out=acc[:, tt * 1024:(tt + 1) * 1024],
                 in0=x1t[sl], in1=y1t[sl], op=OP.add)
            ve_done[sl] = ctx.s_ve.n

        stg1_free = 0
        stg2_free = 0
        b1_free = [0, 0]
        m_w1 = {}
        m_w2 = {}
        m_b1 = {}

        def issue_w1(j):
            nonlocal stg1_free
            e, blk = jobs[j]
            w1d, _, b1d = drams(e)
            g.wait(pe, stg1_free)
            f0 = blk * 1024
            for fs8 in range(8):
                g.dma(stg1[:, fs8 * 1024:(fs8 + 1) * 1024],
                      w1d[f0 + fs8 * 128:f0 + (fs8 + 1) * 128, :], ld2)
            m_w1[j] = ctx.s_ld2.n
            if blk == 0:
                ei = j // 4
                g.wait(pe, b1_free[ei % 2])
                g.dma(b1stg[ei % 2], b1d.rearrange("(c p) -> c p", p=128), ld2)
                m_b1[ei] = ctx.s_ld2.n

        def issue_w2(j):
            nonlocal stg2_free
            e, blk = jobs[j]
            _, w2d, _ = drams(e)
            g.wait(pe, stg2_free)
            f0 = blk * 1024
            for dc8 in range(8):
                g.dma(stg2[:, dc8 * 1024:(dc8 + 1) * 1024],
                      w2d[dc8 * 128:(dc8 + 1) * 128, f0:f0 + 1024], ld2)
            m_w2[j] = ctx.s_ld2.n

        issue_w1(0)
        issue_w2(0)

        gelu_mark = [0, 0]
        part_mark = [0, 0]
        gpm = [0, 0]
        for j, (e, blk) in enumerate(jobs):
            f0 = blk * 1024
            ei = j // 4
            if blk == 0:
                t.wait(ld2, m_b1[ei])
                t.wait(ac, gelu_mark[1])
                t.op("transpose", inc=pe, out=ctx.PS[:, 2560:2592],
                     in_=b1stg[ei % 2], identity=ctx.id_f32[0:32, 0:32])
                b1_free[ei % 2] = ctx.s_pe.n
                v.wait(pe)
                v.wait(ac)
                v.op("tensor_copy", inc=ve, out=ctx.eb1T, in_=ctx.PS[:, 2560:2592])
                s.wait(ve)
            t.wait(ld2, m_w1[j])
            tve = [0, 0]
            for dc in range(NC):
                Tb = T0 if dc % 2 == 0 else T1
                t.wait(ve, tve[dc % 2])
                for fs8 in range(NC):
                    t.op("transpose",
                         out=Tb[:, fs8 * 128:(fs8 + 1) * 128],
                         in_=stg1[:, fs8 * 1024 + dc * 128:fs8 * 1024 + (dc + 1) * 128],
                         identity=ctx.id_bf, inc=(pe if fs8 == NC - 1 else None))
                v.wait(pe)
                v.op("tensor_copy", inc=ve, out=w1T[:, dc * 1024:(dc + 1) * 1024], in_=Tb)
                tve[dc % 2] = ctx.s_ve.n
            stg1_free = ctx.s_pe.n
            if j + 1 < nj:
                issue_w1(j + 1)
            t.wait(ld2, m_w2[j])
            for fc in range(NC):
                Tb = T0 if fc % 2 == 0 else T1
                t.wait(ve, tve[fc % 2])
                for dc8 in range(NC):
                    t.op("transpose",
                         out=Tb[:, dc8 * 128:(dc8 + 1) * 128],
                         in_=stg2[:, dc8 * 1024 + fc * 128:dc8 * 1024 + (fc + 1) * 128],
                         identity=ctx.id_bf, inc=(pe if dc8 == NC - 1 else None))
                v.wait(pe)
                v.op("tensor_copy", inc=ve, out=w2T[:, fc * 1024:(fc + 1) * 1024], in_=Tb)
                tve[fc % 2] = ctx.s_ve.n
            stg2_free = ctx.s_pe.n
            if j + 1 < nj:
                issue_w2(j + 1)
            t.wait(ve)

            for tg in range(4):
                tcol = tg * 512
                for fs in range(8):
                    hb = fs % 2
                    hps = ctx.PS[:, 2048 + hb * 512:2048 + (hb + 1) * 512]
                    t.wait(ac, gelu_mark[hb])
                    for dc in range(NC):
                        t.op("matmul", out=hps,
                             lhsT=w1T[:, dc * 1024 + fs * 128:dc * 1024 + (fs + 1) * 128],
                             rhs=xnTh[:, dc * 2048 + tcol:dc * 2048 + tcol + 512],
                             start=(dc == 0), stop=(dc == NC - 1),
                             inc=(pe if dc == NC - 1 else None))
                    s.wait(pe)
                    s.op("activation", inc=ac, out=hT[:, fs * 512:(fs + 1) * 512],
                         in_=hps, func=AF.Gelu_apprx_tanh,
                         bias=ctx.eb1T[:, blk * 8 + fs:blk * 8 + fs + 1])
                    gelu_mark[hb] = ctx.s_ac.n
                t.wait(ac)
                for sub in range(4):
                    tt = tg * 4 + sub
                    psl = sub % 2
                    po = ctx.PS[:, psl * 1024:(psl + 1) * 1024]
                    t.wait(ac, part_mark[psl])
                    mms = []
                    for fs in range(8):
                        for hh in range(2):
                            mms.append((hT[:, fs * 512 + sub * 128:fs * 512 + sub * 128 + 128],
                                        w2T[:, fs * 1024 + hh * 512:fs * 1024 + (hh + 1) * 512],
                                        po[:, hh * 512:(hh + 1) * 512]))
                    if e is None and blk == 0:
                        for hh in range(2):
                            mms.append((ctx.ones_row,
                                        shb2[:, hh * 512:(hh + 1) * 512],
                                        po[:, hh * 512:(hh + 1) * 512]))
                            mms.append((combT[:, tt * 128:(tt + 1) * 128],
                                        spb2[:, hh * 512:(hh + 1) * 512],
                                        po[:, hh * 512:(hh + 1) * 512]))
                    for i, (lh, rh, oo) in enumerate(mms):
                        t.op("matmul", out=oo, lhsT=lh, rhs=rh,
                             start=(i < 2), stop=(i >= len(mms) - 2),
                             inc=(pe if i == len(mms) - 1 else None))
                    s.wait(pe)
                    s.wait(gp, gpm[psl])
                    if e is None:
                        s.op("activation", inc=ac, out=parts[psl], in_=po,
                             func=AF.Copy)
                    else:
                        gt = th * 16 + tt
                        s.op("activation", inc=ac, out=parts[psl], in_=po,
                             func=AF.Copy,
                             scale=ctx.comb[:, gt * 8 + e:gt * 8 + e + 1])
                    part_mark[psl] = ctx.s_ac.n
                    g.wait(ac)
                    g.op("tensor_tensor", inc=gp,
                         out=acc[:, tt * 1024:(tt + 1) * 1024],
                         in0=parts[psl],
                         in1=acc[:, tt * 1024:(tt + 1) * 1024], op=OP.add)
                    gpm[psl] = ctx.s_gp.n
        outv = ten["out"].rearrange("(th t p) d -> th t p d", th=2, t=16)[th]
        sy.wait(gp)
        sy.dma(outv.transpose([1, 0, 2]),
               acc.rearrange("p (t d) -> p t d", t=16), so)
    sy.wait(so)
    for q in (ctx.qt, ctx.qv, ctx.qs, ctx.qg):
        q.wait(ctx.s_so)


PH_FUNCS = {"ph0": ph0, "ph1": ph1, "ph2": ph2, "ph3": ph3, "ph4": ph4,
            "ph4b": ph4b, "ph5": ph5, "ph6": ph6, "ph7": ph7, "ph8": ph8}

DEBUG_DUMPS = []


def build_group(group_phases):
    nc = bass.Bass("TRN2", target_bir_lowering=False)
    ctx = Ctx()
    ctx.nc = nc
    ctx.ten, input_names = declare_group_tensors(nc, group_phases)
    dump_tens = []
    for nm, shape, dt, fn in DEBUG_DUMPS:
        dump_tens.append((nc.dram_tensor("dbg_" + nm, shape, dt,
                                         kind="ExternalOutput").ap(), fn))
    stack = ExitStack()
    with stack:
        alloc_sbuf(nc, ctx, stack)
        sems = {}
        for nm in ["bar", "ld", "ld2", "pe", "ve", "ac", "so", "gp"]:
            sems[nm] = Sem(stack.enter_context(nc.semaphore(nm)))
        ctx.sem_bar = sems["bar"]
        ctx.s_ld, ctx.s_ld2 = sems["ld"], sems["ld2"]
        ctx.s_pe, ctx.s_ve, ctx.s_ac = sems["pe"], sems["ve"], sems["ac"]
        ctx.s_so, ctx.s_gp = sems["so"], sems["gp"]
        ctx.qt, ctx.qv, ctx.qs = Q("tensor"), Q("vector"), Q("scalar")
        ctx.qg, ctx.qy = Q("gpsimd"), Q("sync")
        ctx.queues = [ctx.qt, ctx.qv, ctx.qs, ctx.qg, ctx.qy]

        for phn in group_phases:
            PH_FUNCS[phn](ctx)
            barrier(ctx)
        for ap, fn in dump_tens:
            ctx.qy.dma(ap, fn(ctx), ctx.s_so)
        ctx.qy.wait(ctx.s_so)

        with nc.Block() as block:
            block.tensor(lambda e: [f(e) for f in ctx.qt.thunks])
            block.vector(lambda e: [f(e) for f in ctx.qv.thunks])
            block.scalar(lambda e: [f(e) for f in ctx.qs.thunks])
            block.gpsimd(lambda e: [f(e) for f in ctx.qg.thunks])
            block.sync(lambda e: [f(e) for f in ctx.qy.thunks])
    return nc, input_names


def run_groups(inputs, groups, n=8):
    x_full = np.ascontiguousarray(inputs["L0_attn_hidden_states"])
    mask = np.where(np.triu(np.ones((128, 128), dtype=bool), 1),
                    np.float32(-1e30), np.float32(0.0)).astype(np.float32)
    const = {"c_ident": np.eye(128, dtype=np.float32), "c_mask": mask}
    weights = {k: np.ascontiguousarray(v, dtype=np.float32)
               for k, v in inputs.items() if k != "L0_attn_hidden_states"}
    carry = [dict() for _ in range(n)]
    for group in groups:
        nc, input_names = build_group(group)
        in_maps = []
        for c in range(n):
            m = {}
            for nm in input_names:
                if nm == "x":
                    m[nm] = np.ascontiguousarray(x_full[c])
                elif nm in const:
                    m[nm] = const[nm]
                elif nm in weights:
                    m[nm] = weights[nm]
                else:
                    m[nm] = carry[c][nm]
            in_maps.append(m)
        res = run_bass_kernel_spmd(nc, in_maps, core_ids=list(range(n)))
        for c in range(n):
            for k, vv in res.results[c].items():
                carry[c][k] = vv
    return carry


def kernel(**inputs):
    carry = run_groups(inputs, GROUPS)
    return np.stack([c["out"] for c in carry])


# version 04
# speedup vs baseline: 1.5401x; 1.0684x over previous
import numpy as np
import ml_dtypes
import concourse.bass as bass
import concourse.mybir as mybir
from concourse.bass_utils import run_bass_kernel_spmd
from contextlib import ExitStack

F32 = mybir.dt.float32
BF16 = mybir.dt.bfloat16
AF = mybir.ActivationFunctionType
OP = mybir.AluOpType
AX = mybir.AxisListType
bf16 = ml_dtypes.bfloat16

T = 4096
D = 1024

DFF = 4096
NH, HD = 16, 64
NT = T // 128
PH6_QI = NT
PH6_PARTS = ("qn", "heads", "exp", "y")
NC = D // 128
NE = 8

SCRATCH = {
    "s_attn0": ([NT, 128, D], BF16),
    "s_ffn0": ([NT, 128, D], BF16),
    "s_x1": ([NT, 128, D], F32),
    "s_xnt1": ([NT, NC, 128, 128], BF16),
    "s_qt": ([NT, NC, 128, 128], BF16),
    "s_y1": ([NT, 128, D], BF16),
}
PHASES = {
    "ph0": (["c_ident", "L0_attn_b_v", "L0_attn_b_o", "L0_ffn_b2",
             "L0_ffn_b1", "L0_ffn_quantum_phases", "L0_ffn_quantum_weights_real",
             "L0_ffn_quantum_weights_imag", "L0_attn_W_basis",
             "L1_ffn_router_w", "L1_ffn_router_b", "L1_attn_ln_g", "L1_attn_ln_b"],
            [], []),
    "ph1": (["x"], [], []),
    "ph2": (["L0_attn_W_v", "L0_attn_W_o"], [], []),
    "ph3": (["c_mask"], [], ["s_attn0"]),
    "ph4": (["L0_ffn_W1", "L0_ffn_W2"], [], ["s_ffn0"]),
    "ph4b": (["x"], ["s_attn0", "s_ffn0"], ["s_x1"]),
    "ph5": (["L1_attn_Wq", "L1_attn_bq", "L1_attn_Wk", "L1_attn_bk",
             "L1_attn_Wv", "L1_attn_bv"], ["s_x1"], ["s_xnt1", "s_qt"]),
    "ph6": ([], ["s_qt"], ["s_y1"]),
    "ph7": ([], [], []),
    "ph8": (["L1_ffn_shared_w1", "L1_ffn_shared_b1", "L1_ffn_shared_w2",
             "L1_ffn_shared_b2", "L1_ffn_spec_w1", "L1_ffn_spec_b1",
             "L1_ffn_spec_w2", "L1_ffn_spec_b2"],
            ["s_x1", "s_y1", "s_xnt1"], []),
}
PHASE_ORDER = ["ph0", "ph1", "ph2", "ph3", "ph4", "ph4b", "ph5", "ph6", "ph7", "ph8"]
GROUPS = [PHASE_ORDER]

INPUT_SPECS = {
    "L0_attn_W_basis": (64, 1024), "L0_attn_W_v": (1024, 1024),
    "L0_attn_b_v": (1024,), "L0_attn_W_o": (1024, 1024), "L0_attn_b_o": (1024,),
    "L0_ffn_W1": (1024, 4096), "L0_ffn_b1": (4096,),
    "L0_ffn_quantum_phases": (4096,), "L0_ffn_quantum_weights_real": (4096,),
    "L0_ffn_quantum_weights_imag": (4096,), "L0_ffn_W2": (4096, 1024),
    "L0_ffn_b2": (1024,), "L1_attn_ln_g": (1024,), "L1_attn_ln_b": (1024,),
    "L1_attn_Wq": (1024, 1024), "L1_attn_bq": (1024,),
    "L1_attn_Wk": (1024, 1024), "L1_attn_bk": (1024,),
    "L1_attn_Wv": (1024, 1024), "L1_attn_bv": (1024,),
    "L1_ffn_shared_w1": (4096, 1024), "L1_ffn_shared_b1": (4096,),
    "L1_ffn_shared_w2": (1024, 4096), "L1_ffn_shared_b2": (1024,),
    "L1_ffn_router_w": (8, 1024), "L1_ffn_router_b": (8,),
    "L1_ffn_spec_w1": (8, 4096, 1024), "L1_ffn_spec_b1": (8, 4096),
    "L1_ffn_spec_w2": (8, 1024, 4096), "L1_ffn_spec_b2": (8, 1024),
    "x": (T, D), "c_ident": (128, 128), "c_mask": (128, 128),
}


class Sem:
    def __init__(self, h):
        self.h = h
        self.n = 0


class Q:

    def __init__(self, name):
        self.name = name
        self.thunks = []

    def raw(self, fn):
        self.thunks.append(fn)

    def wait(self, sem, val=None):
        v = sem.n if val is None else int(val)
        self.raw(lambda e, s=sem.h, v=v: e.wait_ge(s, v))

    def op(self, meth, inc=None, amt=1, **kw):
        if inc is not None:
            inc.n += amt

        def fn(e, meth=meth, kw=kw, inc=inc, amt=amt):
            inst = getattr(e, meth)(**kw)
            if inc is not None:
                inst.then_inc(inc.h, amt)
        self.raw(fn)

    def dma(self, out, in_, inc, **kw):
        inc.n += 16

        def fn(e, out=out, in_=in_, inc=inc, kw=kw):
            e.dma_start(out=out, in_=in_, **kw).then_inc(inc.h, 16)
        self.raw(fn)


class Ctx:
    pass


def declare_group_tensors(nc, group_phases):
    later = []
    idx = max(PHASE_ORDER.index(p) for p in group_phases)
    for p in PHASE_ORDER[idx + 1:]:
        later.append(p)
    later_reads = set()
    for ph in later:
        later_reads |= set(PHASES[ph][1])
    reads, writes, uses = set(), set(), set()
    for ph in group_phases:
        u, r, w = PHASES[ph]
        uses |= set(u)
        reads |= {s for s in r if s not in writes}
        writes |= set(w)
    t = {}
    input_names = []
    for name in sorted(uses):
        t[name] = nc.dram_tensor(name, list(INPUT_SPECS[name]), F32,
                                 kind="ExternalInput").ap()
        input_names.append(name)
    for sname in sorted(reads | writes):
        shape, dt = SCRATCH[sname]
        if sname in reads and sname not in writes:
            kind = "ExternalInput"
            input_names.append(sname)
        elif sname in later_reads:
            kind = "ExternalOutput"
        else:
            kind = "Internal"
        t[sname] = nc.dram_tensor(sname, shape, dt, kind=kind).ap()
    if "ph8" in group_phases:
        t["out"] = nc.dram_tensor("out", [T, D], F32, kind="ExternalOutput").ap()
    return t, input_names


def alloc_sbuf(nc, ctx, stack):
    A = stack.enter_context(nc.sbuf_tensor("arenaA", [128, 32768], BF16))
    B = stack.enter_context(nc.sbuf_tensor("arenaB", [128, 32768], BF16))
    C = stack.enter_context(nc.sbuf_tensor("arenaC", [128, 32768], BF16))
    Dm = stack.enter_context(nc.sbuf_tensor("arenaD", [128, 7936], BF16))
    PS = stack.enter_context(nc.psum_tensor("ps8", [128, 4096], F32))
    ctx.A, ctx.B, ctx.C, ctx.D, ctx.PS = A[:], B[:], C[:], Dm[:], PS[:]
    ctx.PSbf = ctx.PS.bitcast(BF16)
    ctx.id_bf = ctx.D[:, 0:128]
    ctx.id_f32 = ctx.D[:, 128:384].bitcast(F32)
    ctx.ones_row = ctx.D[0:1, 384:512]
    ctx.logits = ctx.D[:, 640:1152].bitcast(F32)
    ctx.comb = ctx.D[:, 1152:1664].bitcast(F32)
    ctx.rows = ctx.D[:, 1664:5760]
    co = 5760

    def colf32(off, n):
        return ctx.D[:, co + off:co + off + 2 * n].bitcast(F32)
    ctx.b1T = colf32(0, 32)
    ctx.sinT = colf32(64, 32)
    ctx.cosT = colf32(128, 32)
    ctx.aT = colf32(192, 32)
    ctx.gT = colf32(256, 8)
    ctx.qbias = colf32(272, 8)
    ctx.kbias = colf32(288, 8)
    ctx.stat = colf32(304, 16)
    ctx.red = colf32(336, 16)
    ctx.tmpc = colf32(368, 64)
    ctx.eb1T = colf32(496, 32)
    ctx.routwT = ctx.D[:, co + 560:co + 624]
    ctx.bT_bf = ctx.D[:, co + 624:co + 632]
    ctx.ones_col = ctx.D[:, co + 632:co + 633]
    ctx.selcol = ctx.D[:, co + 634:co + 636]
    ctx.spare = ctx.D[:, 6784:7936]
    ctx.out0_evac = ctx.spare[:, 0:1024]


def barrier(ctx):
    bar = ctx.sem_bar
    for q in ctx.queues:
        bar.n += 1
        q.raw(lambda e, s=bar.h: e.nop().then_inc(s, 1))
    tgt = bar.n
    for q in ctx.queues:
        q.raw(lambda e, s=bar.h, v=tgt: e.wait_ge(s, v))


def emit_ln(ctx, x_ap, out_bf, sq, eps=1e-5):
    v, s = ctx.qv, ctx.qs
    ve, ac = ctx.s_ve, ctx.s_ac
    st = ctx.stat
    s.wait(ve)
    s.op("activation", inc=ac, out=sq, in_=x_ap, func=AF.Square,
         accum_out=st[:, 0:1], amt=2)
    v.op("tensor_reduce", inc=ve, out=st[:, 1:2], in_=x_ap, axis=AX.X, op=OP.add)
    s.wait(ve)
    s.op("activation", out=st[:, 4:5], in_=st[:, 1:2], func=AF.Square, scale=1.0 / D)
    s.op("activation", inc=ac, out=st[:, 8:9], in_=st[:, 1:2], func=AF.Copy,
         scale=-1.0 / D)
    v.wait(ac)
    v.op("tensor_scalar", inc=ve, out=st[:, 5:6], in0=st[:, 0:1],
         scalar1=1.0 / D, scalar2=st[:, 4:5], op0=OP.mult, op1=OP.subtract)
    s.wait(ve)
    s.op("activation", inc=ac, out=st[:, 6:7], in_=st[:, 5:6], func=AF.Sqrt,
         bias=st[:, 15:16])
    v.wait(ac)
    v.op("reciprocal", inc=ve, out=st[:, 7:8], in_=st[:, 6:7])
    v.wait(ve)
    v.op("tensor_scalar", inc=ve, out=out_bf, in0=x_ap, scalar1=st[:, 8:9],
         scalar2=st[:, 7:8], op0=OP.add, op1=OP.mult)



def ph0(ctx):
    t, v, s, g, sy = ctx.qt, ctx.qv, ctx.qs, ctx.qg, ctx.qy
    ld, pe, ve, ac = ctx.s_ld, ctx.s_pe, ctx.s_ve, ctx.s_ac
    ten = ctx.ten
    C = ctx.C
    f32C = C.bitcast(F32)

    ld2 = ctx.s_ld2
    sy.dma(ctx.id_f32, ten["c_ident"], ld)
    g.dma(ctx.id_bf, ten["c_ident"], ld2)
    g.dma(ctx.rows[0:1, 0:1024], ten["L0_attn_b_v"][None, :], ld2)
    g.dma(ctx.rows[0:1, 1024:2048], ten["L0_attn_b_o"][None, :], ld2)
    g.dma(ctx.rows[0:1, 2048:3072], ten["L0_ffn_b2"][None, :], ld2)
    g.dma(ctx.spare[0:1, 1024:1032], ten["L1_ffn_router_b"][None, :], ld2)
    g.dma(C[0:8, 0:1024], ten["L1_ffn_router_w"], ld2)
    g.dma(C[0:64, 1024:2048], ten["L0_attn_W_basis"], ld2)
    th_st = f32C[0:32, 1024:1152]
    wr_st = f32C[0:32, 1152:1280]
    wi_st = f32C[0:32, 1280:1408]
    b1_st = f32C[0:32, 1408:1536]
    g_st = f32C[0:8, 1536:1664]
    b_st = f32C[0:8, 1664:1792]
    sy.dma(th_st, ten["L0_ffn_quantum_phases"].rearrange("(c p) -> c p", p=128), ld)
    sy.dma(wr_st, ten["L0_ffn_quantum_weights_real"].rearrange("(c p) -> c p", p=128), ld)
    sy.dma(wi_st, ten["L0_ffn_quantum_weights_imag"].rearrange("(c p) -> c p", p=128), ld)
    sy.dma(b1_st, ten["L0_ffn_b1"].rearrange("(c p) -> c p", p=128), ld)
    sy.dma(g_st, ten["L1_attn_ln_g"].rearrange("(c p) -> c p", p=128), ld)
    sy.dma(b_st, ten["L1_attn_ln_b"].rearrange("(c p) -> c p", p=128), ld)

    v.wait(ld)
    v.wait(ld2)
    v.op("memset", ap=ctx.ones_row, constant=1.0)
    v.op("memset", ap=ctx.ones_col, constant=1.0)
    v.op("memset", ap=ctx.selcol, constant=0.0)
    v.op("memset", ap=ctx.selcol[0:64, 0:1], constant=1.0)
    v.op("memset", ap=ctx.selcol[64:128, 1:2], constant=1.0)
    v.op("memset", inc=ve, ap=ctx.stat[:, 15:16], constant=1e-5)

    def vstep(meth, **kw):
        v.op(meth, inc=ve, **kw)
        v.wait(ve)

    PSf = ctx.PS
    t.wait(ld)
    t.wait(ld2)
    t.op("transpose", out=PSf[:, 0:32], in_=th_st, identity=ctx.id_f32[0:32, 0:32])
    t.op("transpose", out=PSf[:, 32:64], in_=wr_st, identity=ctx.id_f32[0:32, 0:32])
    t.op("transpose", out=PSf[:, 64:96], in_=wi_st, identity=ctx.id_f32[0:32, 0:32])
    t.op("transpose", out=PSf[:, 96:128], in_=b1_st, identity=ctx.id_f32[0:32, 0:32])
    t.op("transpose", out=PSf[:, 128:136], in_=g_st, identity=ctx.id_f32[0:8, 0:8])
    t.op("transpose", inc=pe, out=PSf[:, 136:144], in_=b_st, identity=ctx.id_f32[0:8, 0:8])
    v.wait(pe)
    thT = ctx.tmpc[:, 0:32]
    wiT = ctx.tmpc[:, 32:64]
    wrT = ctx.tmpc[:, 96:128] if False else ctx.tmpc[:, 0:0]
    wrT = ctx.red[:, 0:16]
    wrT = ctx.aT
    v.op("tensor_copy", out=thT, in_=PSf[:, 0:32])
    v.op("tensor_copy", out=wrT, in_=PSf[:, 32:64])
    v.op("tensor_copy", out=wiT, in_=PSf[:, 64:96])
    v.op("tensor_copy", out=ctx.b1T, in_=PSf[:, 96:128])
    v.op("tensor_copy", out=ctx.gT, in_=PSf[:, 128:136])
    v.op("tensor_copy", inc=ve, out=ctx.bT_bf, in_=PSf[:, 136:144])
    t.wait(ve)
    v.wait(ve)

    twopi = float(2 * np.pi)
    vstep("scalar_tensor_tensor", out=thT, in0=ctx.b1T, scalar=0.1,
          in1=thT, op0=OP.mult, op1=OP.add)
    msk = ctx.sinT
    for it in range(8):
        vstep("tensor_single_scalar", out=msk, in_=thT, scalar=float(np.pi), op=OP.is_ge)
        vstep("scalar_tensor_tensor", out=thT, in0=msk, scalar=-twopi,
              in1=thT, op0=OP.mult, op1=OP.add)
        vstep("tensor_single_scalar", out=msk, in_=thT, scalar=float(-np.pi), op=OP.is_lt)
        vstep("scalar_tensor_tensor", out=thT, in0=msk, scalar=twopi,
              in1=thT, op0=OP.mult, op1=OP.add)
    s.wait(ve)
    s.op("activation", inc=ac, out=ctx.sinT, in_=thT, func=AF.Sin)
    s.wait(ac)
    s.op("activation", inc=ac, out=ctx.cosT, in_=thT, func=AF.Sin, scale=0.5)
    v.wait(ac)
    vstep("tensor_tensor", out=ctx.cosT, in0=ctx.cosT, in1=ctx.cosT, op=OP.mult)
    vstep("tensor_scalar", out=ctx.cosT, in0=ctx.cosT, scalar1=-2.0, scalar2=1.0,
          op0=OP.mult, op1=OP.add)
    qp = thT
    vstep("tensor_tensor", out=qp, in0=ctx.cosT, in1=wrT, op=OP.mult)
    vstep("tensor_tensor", out=wrT, in0=ctx.sinT, in1=wiT, op=OP.mult)
    vstep("tensor_tensor", out=qp, in0=qp, in1=wrT, op=OP.add)
    vstep("tensor_tensor", out=ctx.aT, in0=qp, in1=wiT, op=OP.mult)
    vstep("tensor_single_scalar", out=ctx.aT, in_=ctx.aT, scalar=0.1, op=OP.mult)

    for c in range(NC):
        t.op("transpose", out=ctx.PSbf[:, 2048 + c * 8:2048 + (c + 1) * 8],
             in_=C[0:8, c * 128:(c + 1) * 128],
             identity=ctx.id_bf[0:8, 0:8], inc=(pe if c == NC - 1 else None))
    v.wait(pe)
    v.op("tensor_copy", inc=ve, out=ctx.routwT, in_=ctx.PSbf[:, 2048:2112])
    for c in range(NC):
        t.op("transpose", out=ctx.PSbf[:, 4096 + c * 64:4096 + (c + 1) * 64],
             in_=C[0:64, 1024 + c * 128:1024 + (c + 1) * 128],
             identity=ctx.id_bf[0:64, 0:64], inc=(pe if c == NC - 1 else None))
    v.wait(pe)
    v.op("tensor_copy", inc=ve, out=C[:, 24576:25088], in_=ctx.PSbf[:, 4096:4608])
    t.wait(ve)
    g.wait(ld2)
    sy.wait(ld)


def ph1(ctx):
    t, v, s, g, sy = ctx.qt, ctx.qv, ctx.qs, ctx.qg, ctx.qy
    ld, pe, ve, ac = ctx.s_ld, ctx.s_pe, ctx.s_ve, ctx.s_ac
    ten = ctx.ten
    A, C = ctx.A, ctx.C
    f32C = C.bitcast(F32)
    sq = f32C[:, 0:1024]
    xr = [f32C[:, 1024:2048], f32C[:, 2048:3072]]
    xn_bf = C[:, 6144:7168]
    ve_done = [0, 0]

    for tt in range(NT):
        sl = tt % 2
        if tt >= 2:
            sy.wait(ctx.s_ve, ve_done[sl])
        sy.dma(xr[sl], ten["x"][tt * 128:(tt + 1) * 128, :], ld)
        s.wait(ld)
        v.wait(ld)
        emit_ln(ctx, xr[sl], xn_bf, sq)
        ve_done[sl] = ctx.s_ve.n
        t.wait(ve)
        for c in range(NC):
            t.op("transpose", out=ctx.PSbf[:, c * 128:(c + 1) * 128],
                 in_=xn_bf[:, c * 128:(c + 1) * 128],
                 identity=ctx.id_bf, inc=(pe if c == NC - 1 else None))
        v.wait(pe)
        for c in range(NC):
            v.op("tensor_copy", out=A[:, c * 4096 + tt * 128:c * 4096 + (tt + 1) * 128],
                 in_=ctx.PSbf[:, c * 128:(c + 1) * 128],
                 inc=(ve if c == NC - 1 else None))
        t.wait(ve)


def _load_and_transpose_1024(ctx, wname, stg, dest):
    t, v, g = ctx.qt, ctx.qv, ctx.qg
    ld2, pe, ve = ctx.s_ld2, ctx.s_pe, ctx.s_ve
    g.wait(pe, getattr(ctx, "stg_free_pe", 0))
    for c in range(NC):
        g.dma(stg[:, c * 1024:(c + 1) * 1024],
              ctx.ten[wname][c * 128:(c + 1) * 128, :], ld2)
    t.wait(ld2)
    for dc in range(NC):
        for oc in range(NC):
            t.op("transpose", out=ctx.PSbf[:, oc * 128:(oc + 1) * 128],
                 in_=stg[:, oc * 1024 + dc * 128:oc * 1024 + (dc + 1) * 128],
                 identity=ctx.id_bf, inc=(pe if oc == NC - 1 else None))
        v.wait(pe)
        v.op("tensor_copy", inc=ve, out=dest[:, dc * 1024:(dc + 1) * 1024],
             in_=ctx.PSbf[:, 0:1024])
        t.wait(ve)
    ctx.stg_free_pe = ctx.s_pe.n


def ph2(ctx):
    t, v, s, g, sy = ctx.qt, ctx.qv, ctx.qs, ctx.qg, ctx.qy
    ld, pe, ve, ac = ctx.s_ld, ctx.s_pe, ctx.s_ve, ctx.s_ac
    ten = ctx.ten
    A, B, C = ctx.A, ctx.B, ctx.C
    stg = C[:, 0:8192]
    WvT = C[:, 8192:16384]
    WoT = C[:, 16384:24576]
    WbT = C[:, 24576:25088]
    latT = C[:, 25088:29184]
    r_bv = ctx.rows[0:1, 0:1024]

    _load_and_transpose_1024(ctx, "L0_attn_W_v", stg, WvT)

    for tt in range(NT):
        for dc in range(NC):
            for h in range(2):
                t.op("matmul", out=ctx.PS[:, h * 512:(h + 1) * 512],
                     lhsT=A[:, dc * 4096 + tt * 128:dc * 4096 + (tt + 1) * 128],
                     rhs=WvT[:, dc * 1024 + h * 512:dc * 1024 + (h + 1) * 512],
                     start=(dc == 0), stop=False)
        for h in range(2):
            t.op("matmul", out=ctx.PS[:, h * 512:(h + 1) * 512],
                 lhsT=ctx.ones_row, rhs=r_bv[:, h * 512:(h + 1) * 512],
                 start=False, stop=True, inc=(pe if h == 1 else None))
        s.wait(pe)
        s.op("activation", inc=ac, out=B[:, tt * 1024:(tt + 1) * 1024],
             in_=ctx.PS[:, 0:1024], func=AF.Copy)
        t.wait(ac)

    _load_and_transpose_1024(ctx, "L0_attn_W_o", stg, WoT)

    for tg in range(8):
        for dc in range(NC):
            t.op("matmul", out=ctx.PS[0:64, 0:512],
                 lhsT=WbT[:, dc * 64:(dc + 1) * 64],
                 rhs=A[:, dc * 4096 + tg * 512:dc * 4096 + (tg + 1) * 512],
                 start=(dc == 0), stop=(dc == NC - 1),
                 inc=(pe if dc == NC - 1 else None))
        v.wait(pe)
        v.op("tensor_copy", inc=ve, out=latT[0:64, tg * 512:(tg + 1) * 512],
             in_=ctx.PS[0:64, 0:512])
        t.wait(ve)


def ph3(ctx):
    t, v, s, g, sy = ctx.qt, ctx.qv, ctx.qs, ctx.qg, ctx.qy
    ld, pe, ve, ac, so = ctx.s_ld, ctx.s_pe, ctx.s_ve, ctx.s_ac, ctx.s_so
    ten = ctx.ten
    A, B, C = ctx.A, ctx.B, ctx.C
    f32C = C.bitcast(F32)
    maskbig = f32C[:, 0:128]
    att = C[:, 8192 + 0:8192 + 4096] if False else C[:, 0:0]
    att = C[:, 4096:8192]
    attT = C[:, 29184 + 0:29184] if False else None
    attT = C[:, 31232 + 0:31232] if False else None
    attT = C[:, 8192:12288]
    WoT = C[:, 16384:24576]
    latT = C[:, 25088:29184]
    y_bf = C[:, 29184:30208]
    yT = C[:, 30208:31232]
    out0 = ctx.out0_evac
    r_bo = ctx.rows[0:1, 1024:2048]
    st_done = 0

    sy.dma(maskbig, ten["c_mask"], ld)
    v.wait(ld)

    for qi in range(NT):
        nk = qi + 1
        klen = nk * 128
        nch = (klen + 511) // 512
        for c in range(nch):
            cols = min(512, klen - c * 512)
            t.op("matmul", out=ctx.PS[:, c * 512:c * 512 + cols],
                 lhsT=latT[0:64, qi * 128:(qi + 1) * 128],
                 rhs=latT[0:64, c * 512:c * 512 + cols],
                 start=True, stop=True, inc=(pe if c == nch - 1 else None))
        v.wait(pe)
        dg = ctx.PS[:, qi * 128:klen]
        v.op("tensor_tensor", inc=ve, out=dg, in0=dg, in1=maskbig, op=OP.add)
        v.wait(ve)
        v.op("tensor_reduce", inc=ve, out=ctx.red[:, 0:1], in_=ctx.PS[:, 0:klen],
             axis=AX.X, op=OP.max, negate=True)
        v.wait(ve)
        v.op("tensor_single_scalar", inc=ve, out=ctx.red[:, 0:1],
             in_=ctx.red[:, 0:1], scalar=0.125, op=OP.mult)
        s.wait(ve)
        s.op("activation", inc=ac, out=att[:, 0:klen], in_=ctx.PS[:, 0:klen],
             func=AF.Exp, scale=0.125, bias=ctx.red[:, 0:1],
             accum_out=ctx.red[:, 1:2], amt=2)
        v.wait(ac)
        v.op("reciprocal", inc=ve, out=ctx.red[:, 2:3], in_=ctx.red[:, 1:2])
        t.wait(ac)
        for kc in range(nk):
            bank = kc // 8
            slot = kc % 8
            t.op("transpose",
                 out=ctx.PSbf[:, bank * 1024 + slot * 128:bank * 1024 + (slot + 1) * 128],
                 in_=att[:, kc * 128:(kc + 1) * 128], identity=ctx.id_bf,
                 inc=(pe if kc == nk - 1 else None))
        nb = (nk + 7) // 8
        v.wait(pe)
        for b in range(nb):
            cols = min(1024, klen - b * 1024)
            v.op("tensor_copy", out=attT[:, b * 1024:b * 1024 + cols],
                 in_=ctx.PSbf[:, b * 1024:b * 1024 + cols],
                 inc=(ve if b == nb - 1 else None))
        t.wait(ve)
        for kc in range(nk):
            for h in range(2):
                t.op("matmul", out=ctx.PS[:, 2048 + h * 512:2048 + (h + 1) * 512],
                     lhsT=attT[:, kc * 128:(kc + 1) * 128],
                     rhs=B[:, kc * 1024 + h * 512:kc * 1024 + (h + 1) * 512],
                     start=(kc == 0), stop=(kc == nk - 1),
                     inc=(pe if (kc == nk - 1 and h == 1) else None))
        s.wait(pe)
        s.wait(ve)
        s.op("activation", inc=ac, out=y_bf, in_=ctx.PS[:, 2048:3072],
             func=AF.Copy, scale=ctx.red[:, 2:3])
        t.wait(ac)
        for c in range(NC):
            t.op("transpose", out=ctx.PSbf[:, c * 128:(c + 1) * 128],
                 in_=y_bf[:, c * 128:(c + 1) * 128], identity=ctx.id_bf,
                 inc=(pe if c == NC - 1 else None))
        v.wait(pe)
        v.op("tensor_copy", inc=ve, out=yT, in_=ctx.PSbf[:, 0:1024])
        t.wait(ve)
        for c in range(NC):
            for h in range(2):
                t.op("matmul", out=ctx.PS[:, 2048 + h * 512:2048 + (h + 1) * 512],
                     lhsT=yT[:, c * 128:(c + 1) * 128],
                     rhs=WoT[:, c * 1024 + h * 512:c * 1024 + (h + 1) * 512],
                     start=(c == 0), stop=False)
        for h in range(2):
            t.op("matmul", out=ctx.PS[:, 2048 + h * 512:2048 + (h + 1) * 512],
                 lhsT=ctx.ones_row, rhs=r_bo[:, h * 512:(h + 1) * 512],
                 start=False, stop=True, inc=(pe if h == 1 else None))
        if qi >= 1:
            s.wait(so, st_done)
        s.wait(pe)
        s.op("activation", inc=ac, out=out0, in_=ctx.PS[:, 2048:3072], func=AF.Copy)
        g.wait(ac)
        g.dma(ten["s_attn0"][qi], out0, so)
        st_done = ctx.s_so.n
        t.wait(ac)
    g.wait(so)


def ph4(ctx):
    t, v, s, g, sy = ctx.qt, ctx.qv, ctx.qs, ctx.qg, ctx.qy
    ld, pe, ve, ac, so = ctx.s_ld, ctx.s_pe, ctx.s_ve, ctx.s_ac, ctx.s_so
    ten = ctx.ten
    A, B, C = ctx.A, ctx.B, ctx.C
    ld2 = ctx.s_ld2
    for c in range(NC):
        g.dma(B[:, c * 4096:(c + 1) * 4096],
              ten["L0_ffn_W1"][c * 128:(c + 1) * 128, :], ld2)
    for fc in range(32):
        g.dma(C[:, fc * 1024:(fc + 1) * 1024],
              ten["L0_ffn_W2"][fc * 128:(fc + 1) * 128, :], ld2)
    t.wait(ld2)

    halfpi = ctx.red[:, 8:9]
    v.op("memset", inc=ve, ap=halfpi, constant=float(np.pi / 2))
    s.wait(ve)
    actT = [ctx.spare[:, 0:256], ctx.spare[:, 256:512]]
    snbuf = ctx.spare[:, 512:1024].bitcast(F32)
    csbuf = ctx.D[:, 6784 + 1024:6784 + 1152].bitcast(F32)
    csbuf = ctx.rows[:, 0:512].bitcast(F32)
    outev = ctx.rows[:, 3072 + 0:3072 + 1024]
    r_b2 = ctx.rows[0:1, 2048:3072]

    for tg in range(16):
        tcol = tg * 256
        for fs in range(32):
            hb = fs % 2
            hps = ctx.PS[:, hb * 512:hb * 512 + 256]
            for dc in range(NC):
                t.op("matmul", out=hps,
                     lhsT=B[:, dc * 4096 + fs * 128:dc * 4096 + (fs + 1) * 128],
                     rhs=A[:, dc * 4096 + tcol:dc * 4096 + tcol + 256],
                     start=(dc == 0), stop=(dc == NC - 1),
                     inc=(pe if dc == NC - 1 else None))
            sincol = ctx.sinT[:, fs:fs + 1]
            coscol = ctx.cosT[:, fs:fs + 1]
            acol = ctx.aT[:, fs:fs + 1]
            b1col = ctx.b1T[:, fs:fs + 1]
            s.wait(pe)
            s.op("activation", out=snbuf, in_=hps, func=AF.Sin, scale=0.1)
            s.op("activation", inc=ac, out=csbuf, in_=hps, func=AF.Sin,
                 scale=-0.1, bias=halfpi)
            v.wait(ac)
            v.op("tensor_scalar", out=snbuf, in0=snbuf, scalar1=coscol,
                 scalar2=None, op0=OP.mult)
            v.op("scalar_tensor_tensor", out=snbuf, in0=csbuf, scalar=sincol,
                 in1=snbuf, op0=OP.mult, op1=OP.add)
            v.op("tensor_scalar", out=snbuf, in0=snbuf, scalar1=acol,
                 scalar2=None, op0=OP.mult)
            v.op("tensor_tensor", inc=ve, out=snbuf, in0=snbuf, in1=hps, op=OP.add)
            s.wait(ve)
            s.op("activation", inc=ac, out=actT[hb], in_=snbuf, func=AF.Relu,
                 bias=b1col)
            t.wait(ac)
            for sub in range(2):
                for h in range(2):
                    t.op("matmul",
                         out=ctx.PS[:, 2048 + sub * 1024 + h * 512:2048 + sub * 1024 + (h + 1) * 512],
                         lhsT=actT[hb][:, sub * 128:(sub + 1) * 128],
                         rhs=C[:, fs * 1024 + h * 512:fs * 1024 + (h + 1) * 512],
                         start=(fs == 0), stop=False)
        for sub in range(2):
            for h in range(2):
                t.op("matmul",
                     out=ctx.PS[:, 2048 + sub * 1024 + h * 512:2048 + sub * 1024 + (h + 1) * 512],
                     lhsT=ctx.ones_row, rhs=r_b2[:, h * 512:(h + 1) * 512],
                     start=False, stop=True,
                     inc=(pe if (sub == 1 and h == 1) else None))
        s.wait(pe)
        for sub in range(2):
            if tg > 0 or sub > 0:
                s.wait(so)
            s.op("activation", inc=ac, out=outev,
                 in_=ctx.PS[:, 2048 + sub * 1024:3072 + sub * 1024], func=AF.Copy)
            g.wait(ac)
            g.dma(ten["s_ffn0"][tg * 2 + sub], outev, so)
        t.wait(ac)
    g.wait(so)


def ph4b(ctx):
    t, v, s, g, sy = ctx.qt, ctx.qv, ctx.qs, ctx.qg, ctx.qy
    ld, ve, so = ctx.s_ld, ctx.s_ve, ctx.s_so
    ten = ctx.ten
    B = ctx.B
    f32B = B.bitcast(F32)
    xr = [f32B[:, 0:1024], f32B[:, 1024:2048]]
    ar = [B[:, 4096:5120], B[:, 5120:6144]]
    fr = [B[:, 6144:7168], B[:, 7168:8192]]
    x1r = [f32B[:, 4096:5120], f32B[:, 5120:6144]]
    ve_done = [0, 0]
    so_done = [0, 0]
    for tt in range(NT):
        sl = tt % 2
        if tt >= 2:
            sy.wait(ve, ve_done[sl])
        sy.dma(xr[sl], ten["x"][tt * 128:(tt + 1) * 128, :], ld)
        sy.dma(ar[sl], ten["s_attn0"][tt], ld)
        sy.dma(fr[sl], ten["s_ffn0"][tt], ld)
        v.wait(ld)
        if tt >= 2:
            v.wait(ctx.s_so, so_done[sl])
        v.op("tensor_tensor", out=xr[sl], in0=xr[sl], in1=ar[sl], op=OP.add)
        v.op("tensor_tensor", inc=ve, out=x1r[sl], in0=xr[sl], in1=fr[sl], op=OP.add)
        ve_done[sl] = ctx.s_ve.n
        g.wait(ve)
        g.dma(ten["s_x1"][tt], x1r[sl], ctx.s_so)
        so_done[sl] = ctx.s_so.n
    g.wait(ctx.s_so)


def ph5(ctx):
    t, v, s, g, sy = ctx.qt, ctx.qv, ctx.qs, ctx.qg, ctx.qy
    ld, pe, ve, ac, so = ctx.s_ld, ctx.s_pe, ctx.s_ve, ctx.s_ac, ctx.s_so
    ten = ctx.ten
    A, B, C = ctx.A, ctx.B, ctx.C
    Wq = C[:, 0:8192]
    Wk = C[:, 8192:16384]
    Wv = C[:, 16384:24576]
    xn_bf = C[:, 24576:25600]
    xnc_bf = C[:, 25600:26624]
    xncT = C[:, 26624:27648]
    xnT_t = C[:, 27648:28672]
    QT_t = C[:, 28672:29696]
    sq = C[:, 29696:31744].bitcast(F32)
    bvp_row = ctx.spare[0:1, 0:1024]
    x1r = [ctx.rows[:, 0:2048].bitcast(F32), ctx.rows[:, 2048:4096].bitcast(F32)]

    ld2 = ctx.s_ld2
    for nm, dst in (("L1_attn_Wq", Wq), ("L1_attn_Wk", Wk), ("L1_attn_Wv", Wv)):
        for c in range(NC):
            g.dma(dst[:, c * 1024:(c + 1) * 1024],
                  ten[nm][c * 128:(c + 1) * 128, :], ld2)
    t.wait(ld2)

    brow = C[0:1, 31744:32768]
    names = [("L1_attn_bq", Wq, "q"), ("L1_attn_bk", Wk, "k"),
             ("L1_attn_bv", Wv, "v")]
    for bn, W, which in names:
        g.wait(pe, ctx.s_pe.n)
        g.dma(brow, ten[bn][None, :], ld2)
        t.wait(ld2)
        for c in range(NC):
            for h in range(2):
                t.op("matmul", out=ctx.PS[0:1, h * 512:(h + 1) * 512],
                     lhsT=ctx.bT_bf[:, c:c + 1],
                     rhs=W[:, c * 1024 + h * 512:c * 1024 + (h + 1) * 512],
                     start=(c == 0), stop=False)
        for h in range(2):
            t.op("matmul", out=ctx.PS[0:1, h * 512:(h + 1) * 512],
                 lhsT=ctx.ones_row[0:1, 0:1], rhs=brow[:, h * 512:(h + 1) * 512],
                 start=False, stop=True, inc=(pe if h == 1 else None))
        if which == "v":
            s.wait(pe)
            s.op("activation", inc=ac, out=bvp_row, in_=ctx.PS[0:1, 0:1024],
                 func=AF.Copy)
            t.wait(ac)
        else:
            v.wait(pe)
            v.op("tensor_copy", inc=ve, out=sq[0:1, 0:1024], in_=ctx.PS[0:1, 0:1024])
            t.wait(ve)
            for c in range(NC):
                t.op("transpose", out=ctx.PS[:, c * 512:c * 512 + 1],
                     in_=sq[0:1, c * 128:(c + 1) * 128],
                     identity=ctx.id_f32[0:1, 0:1],
                     inc=(pe if c == NC - 1 else None))
            v.wait(pe)
            dstcol = ctx.qbias if which == "q" else ctx.kbias
            for c in range(NC):
                v.op("tensor_copy", out=dstcol[:, c:c + 1],
                     in_=ctx.PS[:, c * 512:c * 512 + 1],
                     inc=(ve if c == NC - 1 else None))
            if which == "q":
                v.wait(ve)
                v.op("tensor_single_scalar", inc=ve, out=dstcol, in_=dstcol,
                     scalar=0.125, op=OP.mult)
            t.wait(ve)
    v.wait(ac)
    for W in (Wq, Wk, Wv):
        for c in range(NC):
            v.op("tensor_scalar", out=W[:, c * 1024:(c + 1) * 1024],
                 in0=W[:, c * 1024:(c + 1) * 1024],
                 scalar1=ctx.gT[:, c:c + 1], scalar2=None, op0=OP.mult,
                 inc=(ve if c == NC - 1 else None))
    t.wait(ve)

    ve_done = [0, 0]
    for tt in range(NT):
        sl = tt % 2
        if tt >= 2:
            sy.wait(ve, ve_done[sl])
        sy.dma(x1r[sl], ten["s_x1"][tt], ld)
        s.wait(ld)
        v.wait(ld)
        emit_ln(ctx, x1r[sl], xn_bf, sq)
        ve_done[sl] = ctx.s_ve.n
        s.wait(ve)
        emit_ln(ctx, xn_bf, xnc_bf, sq)
        t.wait(ve)
        for c in range(NC):
            t.op("transpose", out=ctx.PSbf[:, c * 128:(c + 1) * 128],
                 in_=xn_bf[:, c * 128:(c + 1) * 128], identity=ctx.id_bf)
        for c in range(NC):
            t.op("transpose", out=ctx.PSbf[:, 1024 + c * 128:1024 + (c + 1) * 128],
                 in_=xnc_bf[:, c * 128:(c + 1) * 128], identity=ctx.id_bf,
                 inc=(pe if c == NC - 1 else None))
        v.wait(pe)
        v.op("tensor_copy", out=xnT_t, in_=ctx.PSbf[:, 0:1024])
        v.op("tensor_copy", inc=ve, out=xncT, in_=ctx.PSbf[:, 1024:2048])
        g.wait(ve)
        g.dma(ten["s_xnt1"][tt].rearrange("c p f -> p c f"),
              xnT_t.rearrange("p (c f) -> p c f", c=NC), so)
        t.wait(ve)
        for c in range(NC):
            t.op("matmul", out=ctx.PS[:, 2048:2056],
                 lhsT=xnT_t[:, c * 128:(c + 1) * 128],
                 rhs=ctx.routwT[:, c * 8:(c + 1) * 8],
                 start=(c == 0), stop=False)
        t.op("matmul", out=ctx.PS[:, 2048:2056], lhsT=ctx.ones_row,
             rhs=ctx.spare[0:1, 1024:1032], start=False, stop=True, inc=pe)
        v.wait(pe)
        v.op("tensor_copy", inc=ve, out=ctx.logits[:, tt * 8:(tt + 1) * 8],
             in_=ctx.PS[:, 2048:2056])
        for W, bias, is_q in ((Wq, ctx.qbias, True), (Wk, ctx.kbias, False)):
            for oc in range(NC):
                pso = ctx.PS[:, 2560 + (oc % 2) * 128:2560 + (oc % 2) * 128 + 128]
                for c in range(NC):
                    t.op("matmul", out=pso,
                         lhsT=W[:, c * 1024 + oc * 128:c * 1024 + (oc + 1) * 128],
                         rhs=xncT[:, c * 128:(c + 1) * 128],
                         start=(c == 0), stop=(c == NC - 1),
                         inc=(pe if c == NC - 1 else None))
                s.wait(pe)
                if is_q:
                    s.op("activation", inc=ac, out=QT_t[:, oc * 128:(oc + 1) * 128],
                         in_=pso, func=AF.Identity, scale=0.125, bias=bias[:, oc:oc + 1])
                else:
                    s.op("activation", inc=ac,
                         out=A[:, oc * 4096 + tt * 128:oc * 4096 + (tt + 1) * 128],
                         in_=pso, func=AF.Identity, bias=bias[:, oc:oc + 1])
                t.wait(ac)
            if is_q:
                g.wait(ac)
                g.dma(ten["s_qt"][tt].rearrange("c p f -> p c f"),
                      QT_t.rearrange("p (c f) -> p c f", c=NC), so)
        for c in range(NC):
            for h in range(2):
                t.op("matmul", out=ctx.PS[:, h * 512:(h + 1) * 512],
                     lhsT=xncT[:, c * 128:(c + 1) * 128],
                     rhs=Wv[:, c * 1024 + h * 512:c * 1024 + (h + 1) * 512],
                     start=(c == 0), stop=False)
        for h in range(2):
            t.op("matmul", out=ctx.PS[:, h * 512:(h + 1) * 512],
                 lhsT=ctx.ones_row, rhs=bvp_row[:, h * 512:(h + 1) * 512],
                 start=False, stop=True, inc=(pe if h == 1 else None))
        s.wait(pe)
        s.op("activation", inc=ac, out=B[:, tt * 1024:(tt + 1) * 1024],
             in_=ctx.PS[:, 0:1024], func=AF.Copy)
        t.wait(ac)
    g.wait(so)


def ph6(ctx):
    t, v, s, g, sy = ctx.qt, ctx.qv, ctx.qs, ctx.qg, ctx.qy
    ld, pe, ve, ac, so = ctx.s_ld, ctx.s_pe, ctx.s_ve, ctx.s_ac, ctx.s_so
    ten = ctx.ten
    A, B, C = ctx.A, ctx.B, ctx.C
    QTr = [C[:, 0:1024], C[:, 1024:2048]]
    y_t = C[:, 2048:3072]
    Qtm = C[:, 3072:4096]
    Ktm = C[:, 4096:5120]
    scrap = C[:, 5120:5248]
    negb = C[0:1, 12288:14336]
    attTb = [C[:, 14336:18432], C[:, 18432:22528]]
    knmax = ctx.tmpc[:, 0:16]
    maxk2_bc = ctx.tmpc[:, 16:32]
    sqn = ctx.tmpc[:, 32:48]
    bcol = ctx.tmpc[:, 48:64]
    negbcol = ctx.D[:, 6396:6412]
    onesc = ctx.ones_col
    TP = ctx.PSbf[:, 5120:6144]
    RP = ctx.PSbf[:, 6144:8192]

    def tokmajor_sq(src_ap_fn, dst, nheads=16):
        for c in range(NC):
            t.op("transpose", out=TP[:, c * 128:(c + 1) * 128],
                 in_=src_ap_fn(c), identity=ctx.id_bf,
                 inc=(pe if c == NC - 1 else None))
        v.wait(pe)
        v.op("tensor_copy", inc=ve, out=dst, in_=TP)
        t.wait(ve)
        s.wait(ve)
        for h in range(nheads):
            s.op("activation", out=scrap[:, 0:64], in_=dst[:, h * 64:(h + 1) * 64],
                 func=AF.Square, accum_out=sqn[:, h:h + 1],
                 inc=(ac if h == nheads - 1 else None))

    for tt in range(NT):
        tokmajor_sq(lambda c, tt=tt: A[:, c * 4096 + tt * 128:c * 4096 + (tt + 1) * 128],
                    Ktm)
        v.wait(ac)
        if tt == 0:
            v.op("tensor_copy", inc=ve, out=knmax, in_=sqn)
        else:
            v.op("tensor_tensor", inc=ve, out=knmax, in0=knmax, in1=sqn, op=OP.max)
        s.wait(ve)
    t.wait(ve)
    t.op("transpose", inc=pe, out=ctx.PS[0:16, 2048:2176], in_=knmax,
         identity=ctx.id_f32)
    v.wait(pe)
    v.op("tensor_reduce", inc=ve, out=ctx.tmpc[0:16, 48:49],
         in_=ctx.PS[0:16, 2048:2176], axis=AX.X, op=OP.max)
    v.wait(ve)
    t.wait(ve)
    t.op("transpose", inc=pe, out=ctx.PS[0:1, 2560:2576],
         in_=ctx.tmpc[0:16, 48:49], identity=ctx.id_f32[0:16, 0:16])
    v.wait(pe)
    v.op("tensor_copy", inc=ve, out=ctx.tmpc[0:1, 48:64], in_=ctx.PS[0:1, 2560:2576])
    v.wait(ve)
    v.op("tensor_copy", inc=ve, out=negbcol[0:1, 0:16], in_=ctx.tmpc[0:1, 48:64])
    t.wait(ve)
    t.op("matmul", out=ctx.PS[:, 3072:3088], lhsT=ctx.ones_row,
         rhs=negbcol[0:1, 0:16], start=True, stop=True, inc=pe)
    v.wait(pe)
    v.op("tensor_copy", inc=ve, out=maxk2_bc, in_=ctx.PS[:, 3072:3088])
    t.wait(ve)

    pe_done = [0, 0]
    st_done = 0
    exp_mark = [0, 0]
    yevac_mark = 0
    for qi in range(NT):
        sl = qi % 2
        if qi >= 2:
            sy.wait(pe, pe_done[sl])
        sy.dma(QTr[sl], ten["s_qt"][qi].rearrange("c p f -> p c f"), ld)
        t.wait(ld)
        tokmajor_sq(lambda c: QTr[sl][:, c * 128:(c + 1) * 128], Qtm)
        v.wait(ac)
        v.op("tensor_tensor", inc=ve, out=bcol, in0=sqn, in1=maxk2_bc, op=OP.mult)
        s.wait(ve)
        s.op("activation", inc=ac, out=bcol, in_=bcol, func=AF.Sqrt,
             bias=ctx.stat[:, 15:16])
        v.wait(ac)
        v.op("tensor_single_scalar", inc=ve, out=negbcol, in_=bcol,
             scalar=-1.0, op=OP.mult)
        t.wait(ve)
        for h in range(NH):
            t.op("transpose", out=RP[0:1, h * 128:(h + 1) * 128],
                 in_=negbcol[:, h:h + 1], identity=ctx.id_bf,
                 inc=(pe if h == NH - 1 else None))
        v.wait(pe)
        v.op("tensor_copy", inc=ve, out=negb, in_=RP[0:1, 0:2048])
        t.wait(ve)

        expf_mark = [0] * NH
        for hh in range(NH + 1):
            if hh < NH:
                h = hh
                c = h // 2
                base = (h % 2) * 64
                ab = h % 2
                qslice = QTr[sl][base:base + 64, c * 128:(c + 1) * 128]
                nb = negb[:, h * 128:(h + 1) * 128]
                for bg in range(8):
                    bb = bg % 2
                    t.wait(ac, exp_mark[bb])
                    for jj in range(4):
                        kc = bg * 4 + jj
                        pso = ctx.PS[:, bb * 512 + jj * 128:bb * 512 + (jj + 1) * 128]
                        t.op("matmul", out=pso,
                             lhsT=A[base:base + 64, c * 4096 + kc * 128:c * 4096 + (kc + 1) * 128],
                             rhs=qslice, start=True, stop=False)
                        t.op("matmul", out=pso, lhsT=ctx.ones_row, rhs=nb,
                             start=False, stop=True,
                             inc=(pe if jj == 3 else None))
                    s.wait(pe)
                    s.op("activation", inc=ac,
                         out=attTb[ab][:, bg * 512:(bg + 1) * 512],
                         in_=ctx.PS[:, bb * 512:(bb + 1) * 512], func=AF.Exp)
                    exp_mark[bb] = ctx.s_ac.n
                expf_mark[h] = ctx.s_ac.n
                if h == NH - 1:
                    pe_done[sl] = ctx.s_pe.n
            if hh >= 1:
                h = hh - 1
                ab = h % 2
                attT = attTb[ab]
                t.wait(ac, expf_mark[h])
                t.wait(ac, yevac_mark)
                for kc in range(32):
                    t.op("matmul", out=ctx.PS[:, 1024:1088],
                         lhsT=attT[:, kc * 128:(kc + 1) * 128],
                         rhs=B[:, kc * 1024 + h * 64:kc * 1024 + (h + 1) * 64],
                         start=(kc == 0), stop=(kc == 31))
                    t.op("matmul", out=ctx.PS[:, 1536:1537],
                         lhsT=attT[:, kc * 128:(kc + 1) * 128],
                         rhs=onesc[0:128, 0:1],
                         start=(kc == 0), stop=(kc == 31),
                         inc=(pe if kc == 31 else None))
                v.wait(pe)
                v.op("reciprocal", inc=ve, out=ctx.red[:, 2:3], in_=ctx.PS[:, 1536:1537])
                s.wait(ve)
                if h == 0 and qi >= 1:
                    s.wait(so, st_done)
                s.op("activation", inc=ac, out=y_t[:, h * 64:(h + 1) * 64],
                     in_=ctx.PS[:, 1024:1088], func=AF.Copy, scale=ctx.red[:, 2:3])
                yevac_mark = ctx.s_ac.n
        g.wait(ac)
        g.dma(ten["s_y1"][qi], y_t, so)
        st_done = ctx.s_so.n
    g.wait(so)


def ph7(ctx):
    t, v, s, g = ctx.qt, ctx.qv, ctx.qs, ctx.qg
    pe, ve, ac = ctx.s_pe, ctx.s_ve, ctx.s_ac
    lg = ctx.logits
    cb = ctx.comb
    l3 = lg.rearrange("p (t e) -> p t e", e=8)
    c3 = cb.rearrange("p (t e) -> p t e", e=8)
    m1 = ctx.tmpc[:, 0:32]
    m2 = ctx.tmpc[:, 32:64]
    sp = ctx.spare[:, 0:256].bitcast(F32)
    den = sp[:, 0:32]
    w1n = sp[:, 32:64]
    w2n = sp[:, 64:96]
    eq1 = sp[:, 96:128]

    ve = ctx.s_ve

    def vstep(meth, **kw):
        v.op(meth, inc=ve, **kw)
        v.wait(ve)

    s.op("activation", inc=ac, out=lg, in_=lg, func=AF.Exp)
    v.wait(ac)
    vstep("tensor_reduce", out=m2.rearrange("p (t o) -> p t o", o=1),
          in_=l3, axis=AX.X, op=OP.add)
    vstep("reciprocal", out=m2, in_=m2)
    for e in range(8):
        v.op("tensor_tensor", inc=(ve if e == 7 else None),
             out=l3[:, :, e], in0=l3[:, :, e], in1=m2, op=OP.mult)
    v.wait(ve)
    vstep("tensor_reduce", out=m1.rearrange("p (t o) -> p t o", o=1),
          in_=l3, axis=AX.X, op=OP.max)
    for e in range(8):
        vstep("tensor_tensor", out=c3[:, :, e], in0=l3[:, :, e], in1=m1, op=OP.is_equal)
        vstep("scalar_tensor_tensor", out=c3[:, :, e], in0=c3[:, :, e],
              scalar=-1e9, in1=l3[:, :, e], op0=OP.mult, op1=OP.add)
    vstep("tensor_reduce", out=m2.rearrange("p (t o) -> p t o", o=1),
          in_=c3, axis=AX.X, op=OP.max)
    vstep("tensor_tensor", out=den, in0=m1, in1=m2, op=OP.add)
    vstep("reciprocal", out=den, in_=den)
    vstep("tensor_tensor", out=w1n, in0=m1, in1=den, op=OP.mult)
    vstep("tensor_tensor", out=w2n, in0=m2, in1=den, op=OP.mult)
    for e in range(8):
        vstep("tensor_tensor", out=eq1, in0=l3[:, :, e], in1=m1, op=OP.is_equal)
        vstep("tensor_tensor", out=eq1, in0=eq1, in1=w1n, op=OP.mult)
        vstep("tensor_tensor", out=c3[:, :, e], in0=c3[:, :, e], in1=m2, op=OP.is_equal)
        vstep("tensor_tensor", out=c3[:, :, e], in0=c3[:, :, e], in1=w2n, op=OP.mult)
        vstep("tensor_tensor", out=c3[:, :, e], in0=c3[:, :, e], in1=eq1, op=OP.add)


def ph8(ctx):
    t, v, s, g, sy = ctx.qt, ctx.qv, ctx.qs, ctx.qg, ctx.qy
    ld, pe, ve, ac, so = ctx.s_ld, ctx.s_pe, ctx.s_ve, ctx.s_ac, ctx.s_so
    ld2, gp = ctx.s_ld2, ctx.s_gp
    ten = ctx.ten
    A, B, C = ctx.A, ctx.B, ctx.C
    acc = C.bitcast(F32)
    xnTh = A[:, 0:16384]
    w1T = A[:, 16384:24576]
    w2T = A[:, 24576:32768]
    stg1 = B[:, 0:8192]
    stg2 = B[:, 8192:16384]
    hT = B[:, 16384:20480]
    parts = [B[:, 20480:22528].bitcast(F32), B[:, 22528:24576].bitcast(F32)]
    x1t = [B[:, 24576:26624].bitcast(F32), B[:, 26624:28672].bitcast(F32)]
    y1t = [B[:, 28672:29696], B[:, 29696:30720]]
    combT = B[0:8, 30720:32768]
    f32D = ctx.D.bitcast(F32)
    spb2 = ctx.rows[0:8, 0:1024]
    shb2 = ctx.rows[0:1, 1024:2048]
    b1stg = [f32D[0:32, 1856:1984], f32D[0:32, 1984:2112]]
    T0 = ctx.PSbf[:, 6144:7168]
    T1 = ctx.PSbf[:, 7168:8192]

    g.dma(spb2, ten["L1_ffn_spec_b2"], ld2)
    g.dma(shb2, ten["L1_ffn_shared_b2"][None, :], ld2)

    ACTS = [("sh", None)] + [("e%d" % e, e) for e in range(NE)]

    def drams(e):
        if e is None:
            return (ten["L1_ffn_shared_w1"], ten["L1_ffn_shared_w2"],
                    ten["L1_ffn_shared_b1"])
        return (ten["L1_ffn_spec_w1"][e], ten["L1_ffn_spec_w2"][e],
                ten["L1_ffn_spec_b1"][e])

    for th in range(2):
        jobs = [(e, blk) for _, e in ACTS for blk in range(4)]
        nj = len(jobs)
        sy.wait(pe, ctx.s_pe.n)
        for c in range(NC):
            sy.dma(xnTh[:, c * 2048:(c + 1) * 2048].rearrange("p (t f) -> p t f", t=16),
                   ten["s_xnt1"][th * 16:(th + 1) * 16, c].transpose([1, 0, 2]), ld)
        t.wait(ve)
        t.wait(gp)
        for tt in range(16):
            gt = th * 16 + tt
            t.op("transpose", out=ctx.PS[0:8, 2048 + tt * 128:2048 + (tt + 1) * 128],
                 in_=ctx.comb[:, gt * 8:(gt + 1) * 8],
                 identity=ctx.id_f32, inc=(pe if tt == 15 else None))
        v.wait(pe)
        v.op("tensor_copy", inc=ve, out=combT, in_=ctx.PS[0:8, 2048:4096])
        t.wait(ve)
        v.wait(so)
        ve_done = [0, 0]
        sy.wait(ve, ctx.s_ve.n)
        for tt in range(16):
            sl = tt % 2
            gt = th * 16 + tt
            if tt >= 2:
                sy.wait(ve, ve_done[sl])
            sy.dma(x1t[sl], ten["s_x1"][gt], ld)
            sy.dma(y1t[sl], ten["s_y1"][gt], ld)
            v.wait(ld)
            v.op("tensor_tensor", inc=ve, out=acc[:, tt * 1024:(tt + 1) * 1024],
                 in0=x1t[sl], in1=y1t[sl], op=OP.add)
            ve_done[sl] = ctx.s_ve.n

        stg1_free = 0
        stg2_free = 0
        b1_free = [0, 0]
        m_w1 = {}
        m_w2 = {}
        m_b1 = {}

        def issue_w1(j):
            nonlocal stg1_free
            e, blk = jobs[j]
            w1d, _, b1d = drams(e)
            g.wait(pe, stg1_free)
            f0 = blk * 1024
            for fs8 in range(8):
                g.dma(stg1[:, fs8 * 1024:(fs8 + 1) * 1024],
                      w1d[f0 + fs8 * 128:f0 + (fs8 + 1) * 128, :], ld2)
            m_w1[j] = ctx.s_ld2.n
            if blk == 0:
                ei = j // 4
                g.wait(pe, b1_free[ei % 2])
                g.dma(b1stg[ei % 2], b1d.rearrange("(c p) -> c p", p=128), ld2)
                m_b1[ei] = ctx.s_ld2.n

        def issue_w2(j):
            nonlocal stg2_free
            e, blk = jobs[j]
            _, w2d, _ = drams(e)
            g.wait(pe, stg2_free)
            f0 = blk * 1024
            for dc8 in range(8):
                g.dma(stg2[:, dc8 * 1024:(dc8 + 1) * 1024],
                      w2d[dc8 * 128:(dc8 + 1) * 128, f0:f0 + 1024], ld2)
            m_w2[j] = ctx.s_ld2.n

        issue_w1(0)
        issue_w2(0)

        gelu_mark = [0, 0]
        part_mark = [0, 0]
        gpm = [0, 0]
        for j, (e, blk) in enumerate(jobs):
            f0 = blk * 1024
            ei = j // 4
            if blk == 0:
                t.wait(ld2, m_b1[ei])
                t.wait(ac, gelu_mark[1])
                t.op("transpose", inc=pe, out=ctx.PS[:, 2560:2592],
                     in_=b1stg[ei % 2], identity=ctx.id_f32[0:32, 0:32])
                b1_free[ei % 2] = ctx.s_pe.n
                v.wait(pe)
                v.wait(ac)
                v.op("tensor_copy", inc=ve, out=ctx.eb1T, in_=ctx.PS[:, 2560:2592])
                s.wait(ve)
            t.wait(ld2, m_w1[j])
            tve = [0, 0]
            for dc in range(NC):
                Tb = T0 if dc % 2 == 0 else T1
                t.wait(ve, tve[dc % 2])
                for fs8 in range(NC):
                    t.op("transpose",
                         out=Tb[:, fs8 * 128:(fs8 + 1) * 128],
                         in_=stg1[:, fs8 * 1024 + dc * 128:fs8 * 1024 + (dc + 1) * 128],
                         identity=ctx.id_bf, inc=(pe if fs8 == NC - 1 else None))
                v.wait(pe)
                v.op("tensor_copy", inc=ve, out=w1T[:, dc * 1024:(dc + 1) * 1024], in_=Tb)
                tve[dc % 2] = ctx.s_ve.n
            stg1_free = ctx.s_pe.n
            if j + 1 < nj:
                issue_w1(j + 1)
            t.wait(ld2, m_w2[j])
            for fc in range(NC):
                Tb = T0 if fc % 2 == 0 else T1
                t.wait(ve, tve[fc % 2])
                for dc8 in range(NC):
                    t.op("transpose",
                         out=Tb[:, dc8 * 128:(dc8 + 1) * 128],
                         in_=stg2[:, dc8 * 1024 + fc * 128:dc8 * 1024 + (fc + 1) * 128],
                         identity=ctx.id_bf, inc=(pe if dc8 == NC - 1 else None))
                v.wait(pe)
                v.op("tensor_copy", inc=ve, out=w2T[:, fc * 1024:(fc + 1) * 1024], in_=Tb)
                tve[fc % 2] = ctx.s_ve.n
            stg2_free = ctx.s_pe.n
            if j + 1 < nj:
                issue_w2(j + 1)
            t.wait(ve)

            for tg in range(4):
                tcol = tg * 512
                for fs in range(8):
                    hb = fs % 2
                    hps = ctx.PS[:, 2048 + hb * 512:2048 + (hb + 1) * 512]
                    t.wait(ac, gelu_mark[hb])
                    for dc in range(NC):
                        t.op("matmul", out=hps,
                             lhsT=w1T[:, dc * 1024 + fs * 128:dc * 1024 + (fs + 1) * 128],
                             rhs=xnTh[:, dc * 2048 + tcol:dc * 2048 + tcol + 512],
                             start=(dc == 0), stop=(dc == NC - 1),
                             inc=(pe if dc == NC - 1 else None))
                    s.wait(pe)
                    s.op("activation", inc=ac, out=hT[:, fs * 512:(fs + 1) * 512],
                         in_=hps, func=AF.Gelu_apprx_tanh,
                         bias=ctx.eb1T[:, blk * 8 + fs:blk * 8 + fs + 1])
                    gelu_mark[hb] = ctx.s_ac.n
                t.wait(ac)
                for sub in range(4):
                    tt = tg * 4 + sub
                    psl = sub % 2
                    po = ctx.PS[:, psl * 1024:(psl + 1) * 1024]
                    t.wait(ac, part_mark[psl])
                    mms = []
                    for fs in range(8):
                        for hh in range(2):
                            mms.append((hT[:, fs * 512 + sub * 128:fs * 512 + sub * 128 + 128],
                                        w2T[:, fs * 1024 + hh * 512:fs * 1024 + (hh + 1) * 512],
                                        po[:, hh * 512:(hh + 1) * 512]))
                    if e is None and blk == 0:
                        for hh in range(2):
                            mms.append((ctx.ones_row,
                                        shb2[:, hh * 512:(hh + 1) * 512],
                                        po[:, hh * 512:(hh + 1) * 512]))
                            mms.append((combT[:, tt * 128:(tt + 1) * 128],
                                        spb2[:, hh * 512:(hh + 1) * 512],
                                        po[:, hh * 512:(hh + 1) * 512]))
                    for i, (lh, rh, oo) in enumerate(mms):
                        t.op("matmul", out=oo, lhsT=lh, rhs=rh,
                             start=(i < 2), stop=(i >= len(mms) - 2),
                             inc=(pe if i == len(mms) - 1 else None))
                    s.wait(pe)
                    s.wait(gp, gpm[psl])
                    if e is None:
                        s.op("activation", inc=ac, out=parts[psl], in_=po,
                             func=AF.Copy)
                    else:
                        gt = th * 16 + tt
                        s.op("activation", inc=ac, out=parts[psl], in_=po,
                             func=AF.Copy,
                             scale=ctx.comb[:, gt * 8 + e:gt * 8 + e + 1])
                    part_mark[psl] = ctx.s_ac.n
                    g.wait(ac)
                    g.op("tensor_tensor", inc=gp,
                         out=acc[:, tt * 1024:(tt + 1) * 1024],
                         in0=parts[psl],
                         in1=acc[:, tt * 1024:(tt + 1) * 1024], op=OP.add)
                    gpm[psl] = ctx.s_gp.n
        outv = ten["out"].rearrange("(th t p) d -> th t p d", th=2, t=16)[th]
        sy.wait(gp)
        sy.dma(outv.transpose([1, 0, 2]),
               acc.rearrange("p (t d) -> p t d", t=16), so)
    sy.wait(so)
    for q in (ctx.qt, ctx.qv, ctx.qs, ctx.qg):
        q.wait(ctx.s_so)


PH_FUNCS = {"ph0": ph0, "ph1": ph1, "ph2": ph2, "ph3": ph3, "ph4": ph4,
            "ph4b": ph4b, "ph5": ph5, "ph6": ph6, "ph7": ph7, "ph8": ph8}

DEBUG_DUMPS = []


def build_group(group_phases):
    nc = bass.Bass("TRN2", target_bir_lowering=False)
    ctx = Ctx()
    ctx.nc = nc
    ctx.ten, input_names = declare_group_tensors(nc, group_phases)
    dump_tens = []
    for nm, shape, dt, fn in DEBUG_DUMPS:
        dump_tens.append((nc.dram_tensor("dbg_" + nm, shape, dt,
                                         kind="ExternalOutput").ap(), fn))
    stack = ExitStack()
    with stack:
        alloc_sbuf(nc, ctx, stack)
        sems = {}
        for nm in ["bar", "ld", "ld2", "pe", "ve", "ac", "so", "gp"]:
            sems[nm] = Sem(stack.enter_context(nc.semaphore(nm)))
        ctx.sem_bar = sems["bar"]
        ctx.s_ld, ctx.s_ld2 = sems["ld"], sems["ld2"]
        ctx.s_pe, ctx.s_ve, ctx.s_ac = sems["pe"], sems["ve"], sems["ac"]
        ctx.s_so, ctx.s_gp = sems["so"], sems["gp"]
        ctx.qt, ctx.qv, ctx.qs = Q("tensor"), Q("vector"), Q("scalar")
        ctx.qg, ctx.qy = Q("gpsimd"), Q("sync")
        ctx.queues = [ctx.qt, ctx.qv, ctx.qs, ctx.qg, ctx.qy]

        for phn in group_phases:
            PH_FUNCS[phn](ctx)
            barrier(ctx)
        for ap, fn in dump_tens:
            ctx.qy.dma(ap, fn(ctx), ctx.s_so)
        ctx.qy.wait(ctx.s_so)

        with nc.Block() as block:
            block.tensor(lambda e: [f(e) for f in ctx.qt.thunks])
            block.vector(lambda e: [f(e) for f in ctx.qv.thunks])
            block.scalar(lambda e: [f(e) for f in ctx.qs.thunks])
            block.gpsimd(lambda e: [f(e) for f in ctx.qg.thunks])
            block.sync(lambda e: [f(e) for f in ctx.qy.thunks])
    return nc, input_names


def run_groups(inputs, groups, n=8):
    x_full = np.ascontiguousarray(inputs["L0_attn_hidden_states"])
    mask = np.where(np.triu(np.ones((128, 128), dtype=bool), 1),
                    np.float32(-1e30), np.float32(0.0)).astype(np.float32)
    const = {"c_ident": np.eye(128, dtype=np.float32), "c_mask": mask}
    weights = {k: np.ascontiguousarray(v, dtype=np.float32)
               for k, v in inputs.items() if k != "L0_attn_hidden_states"}
    carry = [dict() for _ in range(n)]
    for group in groups:
        nc, input_names = build_group(group)
        in_maps = []
        for c in range(n):
            m = {}
            for nm in input_names:
                if nm == "x":
                    m[nm] = np.ascontiguousarray(x_full[c])
                elif nm in const:
                    m[nm] = const[nm]
                elif nm in weights:
                    m[nm] = weights[nm]
                else:
                    m[nm] = carry[c][nm]
            in_maps.append(m)
        res = run_bass_kernel_spmd(nc, in_maps, core_ids=list(range(n)))
        for c in range(n):
            for k, vv in res.results[c].items():
                carry[c][k] = vv
    return carry


def kernel(**inputs):
    carry = run_groups(inputs, GROUPS)
    return np.stack([c["out"] for c in carry])
